# version 01
# baseline (speedup 1.0000x reference)
import contextlib
import os
import numpy as np
import ml_dtypes
import concourse.bass as bass
import concourse.mybir as mybir
from concourse.bass_utils import run_bass_kernel_spmd
from concourse.alu_op_type import AluOpType as OP

F32 = mybir.dt.float32
ENG4 = None
BF16 = mybir.dt.bfloat16
AF = mybir.ActivationFunctionType
AX = mybir.AxisListType

D = 1024
S = 2048
NHEAD = 16
HD = 64
T = 512
NT = S // T
NSUB = T // 128
NEG = -1.0e38
EPS = 1.0e-5

BUILD_ATTN = True
BUILD_FFN = True

SWAP_MASK = []
for _i in range(16):
    SWAP_MASK += [2 * _i + 1, 2 * _i]


def _rope_tables():
    inv_freq = 1.0 / (10000.0 ** (np.arange(0, HD, 2, dtype=np.float64) / HD))
    pos = np.arange(S, dtype=np.float64)
    freqs = np.outer(pos, inv_freq)
    emb = np.concatenate([freqs, freqs], axis=-1)
    cos = np.cos(emb).astype(np.float32)
    sin = np.sin(emb).astype(np.float32)
    cosT = np.ascontiguousarray(cos.T)
    sinT = np.ascontiguousarray(sin.T)
    sinPM = sinT.copy()
    sinPM[0::2, :] *= -1.0
    cos128 = np.ascontiguousarray(np.concatenate([cosT, cosT], axis=0))
    sin128 = np.ascontiguousarray(np.concatenate([sinPM, sinPM], axis=0))
    return cos128, sin128


def _mask_table():
    m = np.zeros((16, 128, 260), dtype=np.float32)
    r = np.arange(128)[:, None]
    c = np.arange(128)[None, :]
    lower = (c <= r)
    upper = (c >= r)
    for t in range(16):
        m[t, :, 128:256] = np.where(lower, 0.0, NEG)
        if t == 0:
            m[t, :, 0:128] = NEG
        elif t % 2 == 1:
            m[t, :, 0:128] = 0.0
        else:
            m[t, :, 0:128] = np.where(upper, 0.0, NEG)
        m[t, :, 256:260] = 0.0 if t >= 2 else NEG
    return m


def _tape(w):
    MC = w.shape[0] // 128
    KC = w.shape[1] // 128
    t = w.reshape(MC, 128, KC, 128).transpose(0, 3, 2, 1)
    return np.ascontiguousarray(t.reshape(MC, 128, KC * 128))


def _scale_cols(s):
    MC = s.shape[0] // 128
    return np.ascontiguousarray(s.reshape(MC, 128).T)


class Sem:
    def __init__(self, nc, es, name):
        self.h = es.enter_context(nc.semaphore(name))
        self.c = 0

    def inc(self, n=1):
        self.c += n
        return self.c

    def reset(self):
        self.c = 0


def _hilo(vec, out_hi, out_lo, src, scale=None):
    vec.drain()
    if scale is None:
        vec.tensor_copy(out=out_hi, in_=src)
        vec.drain()
        return vec.scalar_tensor_tensor(out=out_lo, in0=src, scalar=1.0, in1=out_hi,
                                        op0=OP.mult, op1=OP.subtract)
    vec.tensor_scalar(out=out_hi, in0=src, scalar1=scale, scalar2=None, op0=OP.mult)
    vec.drain()
    return vec.scalar_tensor_tensor(out=out_lo, in0=src, scalar=scale, in1=out_hi,
                                    op0=OP.mult, op1=OP.subtract)



def _emit_ln(nc, x_sb, dst_hi, dst_lo, scr, consts, banks, sT, sV, sS, x_sem, x_cnt,
             w8=None):
    if w8 is None:
        def w8(eng, sem, cnt):
            eng.wait_ge(sem.h, cnt)
    m_sb, sq_sb, var_sb, sd_sb, a_sb, b_sb, tmpA, tmpB = scr
    ones_c, ones_r, eps_ap = consts
    bS1, bS2, bA, bB = banks
    base_T = sT.c
    base_S = sS.c
    for c in range(8):
        if c == 0:
            w8(nc.scalar, x_sem, x_cnt)
        else:
            w8(nc.scalar, sT, base_T + c)
        nc.scalar.activation(out=tmpA, in_=x_sb.ap()[:, c, :],
                             func=AF.Square).then_inc(sS.h, 1)
        sS.inc()
        w8(nc.tensor, sS, base_S + c + 1)
        nc.tensor.matmul(out=bS1.ap()[:1, :], lhsT=ones_c, rhs=x_sb.ap()[:, c, :],
                         start=(c == 0), stop=(c == 7))
        nc.tensor.matmul(out=bS2.ap()[:1, :], lhsT=ones_c, rhs=tmpA,
                         start=(c == 0), stop=(c == 7)).then_inc(sT.h, 1)
        sT.inc()
    w8(nc.vector, sT, sT.c)
    nc.vector.tensor_scalar(out=m_sb, in0=bS1.ap()[:1, :], scalar1=1.0 / D,
                            scalar2=None, op0=OP.mult)
    nc.vector.tensor_scalar(out=sq_sb, in0=bS2.ap()[:1, :], scalar1=1.0 / D,
                            scalar2=None, op0=OP.mult).then_inc(sV.h, 1)
    sV.inc()
    w8(nc.scalar, sV, sV.c)
    nc.scalar.activation(out=var_sb, in_=m_sb, func=AF.Square).then_inc(sS.h, 1)
    sS.inc()
    w8(nc.vector, sS, sS.c)
    nc.vector.tensor_sub(out=var_sb, in0=sq_sb, in1=var_sb).then_inc(sV.h, 1)
    sV.inc()
    w8(nc.scalar, sV, sV.c)
    nc.scalar.activation(out=sd_sb, in_=var_sb, func=AF.Sqrt, bias=eps_ap
                         ).then_inc(sS.h, 1)
    sS.inc()
    w8(nc.vector, sS, sS.c)
    nc.vector.reciprocal(out=a_sb, in_=sd_sb)
    nc.vector.drain()
    nc.vector.scalar_tensor_tensor(out=b_sb, in0=m_sb, scalar=-1.0, in1=a_sb,
                                   op0=OP.mult, op1=OP.mult).then_inc(sV.h, 1)
    sV.inc()
    w8(nc.tensor, sV, sV.c)
    nc.tensor.matmul(out=bA.ap()[:], lhsT=ones_r, rhs=a_sb, start=True, stop=True)
    nc.tensor.matmul(out=bB.ap()[:], lhsT=ones_r, rhs=b_sb, start=True,
                     stop=True).then_inc(sT.h, 1)
    sT.inc()
    w8(nc.vector, sT, sT.c)
    for c in range(8):
        nc.vector.tensor_mul(out=tmpB, in0=x_sb.ap()[:, c, :], in1=bA.ap()[:])
        nc.vector.drain()
        nc.vector.tensor_add(out=tmpA, in0=tmpB, in1=bB.ap()[:])
        _hilo(nc.vector, dst_hi.ap()[:, c, :], dst_lo.ap()[:, c, :], tmpA
              ).then_inc(sV.h, 1)
        sV.inc()



def _emit_attention_tile(nc, it, dma, loop_seal, loop_sems, sm, dr, b, pb, pt6, pt7,
                         regs):
    aX, aM, aQW, aOW = sm["aX"], sm["aM"], sm["aQW"], sm["aOW"]
    aT, aV, aSc, aSP = sm["aT"], sm["aV"], sm["aSc"], sm["aSP"]
    r_s1, r_s2, r_v1 = regs
    kbh = b["kb_hi"].ap()
    kbl = b["kb_lo"].ap()
    vbh = b["vb_hi"].ap()
    vbl = b["vb_lo"].ap()

    nc.sync.reg_mov(r_s1, it)
    nc.sync.reg_mul(r_s1, r_s1, T)
    dma(nc.sync, aX, b["x_sb"].ap(),
        bass.AP(dr["x_d"], r_s1, [[S, 128], [128 * S, 8], [1, T]]))
    dma(nc.sync, aM, b["ropeC"].ap(), bass.AP(dr["cos_d"], r_s1, [[S, 128], [1, T]]))
    dma(nc.sync, aM, b["ropeS"].ap(), bass.AP(dr["sin_d"], r_s1, [[S, 128], [1, T]]))
    nc.sync.reg_mov(r_s2, it)
    nc.sync.reg_mul(r_s2, r_s2, 4 * 128 * 260)
    dma(nc.sync, aM, b["mask_sb"].ap(),
        bass.AP(dr["mask_d"], r_s2, [[260, 128], [128 * 260, 4], [1, 260]]))

    ln_scr = (b["ln_m"].ap()[:], b["ln_sq"].ap()[:], b["ln_var"].ap()[:],
              b["ln_sd"].ap()[:], b["ln_a"].ap()[:], b["ln_b"].ap()[:],
              b["sc_u"].ap()[:], b["sc_qs"].ap()[:])
    _emit_ln(nc, b["x_sb"], b["xn_hi"], b["xn_lo"], ln_scr,
             (b["ones_c"].ap()[:], b["ones_r"].ap()[:], b["eps_sb"].ap()[:]),
             (pb[2], pb[3], pb[4], pb[5]), aT, aV, aSc, aX, 16)
    mark_ln = aV.c
    aT_after_ln = aT.c

    for g in range(8):
        if g >= 2:
            nc.sync.wait_ge(aT.h, aT_after_ln + 2 * (g - 1))
        dma(nc.sync, aQW[g % 2], b["qkw_hi"].ap()[:, g % 2],
            dr["qh_r"][:, :, 256 * g:256 * (g + 1)])
        dma(nc.sync, aQW[g % 2], b["qkw_lo"].ap()[:, g % 2],
            dr["ql_r"][:, :, 256 * g:256 * (g + 1)])

    mark_qk_evac = []
    for m in range(16):
        bank = pb[m % 2]
        grp = m // 2
        nc.tensor.wait_ge(aQW[grp % 2].h, 32 * (grp // 2 + 1))
        if m == 0:
            nc.tensor.wait_ge(aV.h, mark_ln)
        if m >= 2:
            nc.tensor.wait_ge(aV.h, mark_qk_evac[m - 2])
        for k in range(8):
            lh = b["qkw_hi"].ap()[:, grp % 2, k, (m % 2) * 128:(m % 2) * 128 + 128]
            ll = b["qkw_lo"].ap()[:, grp % 2, k, (m % 2) * 128:(m % 2) * 128 + 128]
            xh = b["xn_hi"].ap()[:, k, :]
            xl = b["xn_lo"].ap()[:, k, :]
            nc.tensor.matmul(out=bank.ap()[:], lhsT=lh, rhs=xh, start=(k == 0),
                             stop=False)
            nc.tensor.matmul(out=bank.ap()[:], lhsT=lh, rhs=xl, start=False, stop=False)
            mm = nc.tensor.matmul(out=bank.ap()[:], lhsT=ll, rhs=xh, start=False,
                                  stop=(k == 7))
        mm.then_inc(aT.h, 1)
        aT.inc()
        nc.vector.wait_ge(aT.h, aT.c)
        if m == 0:
            nc.vector.wait_ge(aM.h, 48)
        q32, qs, u = b["sc_q32"].ap()[:], b["sc_qs"].ap()[:], b["sc_u"].ap()[:]
        v2 = b["sc_v"].ap()[:]
        nc.vector.drain()
        nc.vector.tensor_copy(out=q32, in_=bank.ap()[:])
        nc.vector.drain()
        nc.vector.stream_shuffle(out=qs, in_=q32, mask=SWAP_MASK)
        nc.vector.tensor_mul(out=u, in0=q32, in1=b["ropeC"].ap()[:])
        nc.vector.drain()
        nc.vector.tensor_mul(out=v2, in0=qs, in1=b["ropeS"].ap()[:])
        nc.vector.drain()
        nc.vector.tensor_add(out=q32, in0=u, in1=v2)
        if m < 8:
            last = _hilo(nc.vector, b["qp_hi"].ap()[:, m, :], b["qp_lo"].ap()[:, m, :],
                         q32)
        else:
            c = m - 8
            for s in range(NSUB):
                slot = 128 * (s + 1)
                last = _hilo(nc.vector, kbh[:, c, slot:slot + 128],
                             kbl[:, c, slot:slot + 128],
                             b["sc_q32"].ap()[:, 128 * s:128 * (s + 1)])
        last.then_inc(aV.h, 1)
        aV.inc()
        mark_qk_evac.append(aV.c)
    nc.vector.drain()
    nc.vector.reg_mov(r_v1, it)
    with nc.vector.If_eq(r_v1, 0):
        for c in range(8):
            nc.vector.tensor_copy(out=kbh[:, c, 640:644], in_=kbh[:, c, 128:132])
            nc.vector.tensor_copy(out=kbl[:, c, 640:644], in_=kbl[:, c, 128:132])
    nc.vector.drain()
    nc.vector.sem_inc(aV.h, 1)
    aV.inc()
    mark_k_ready = aV.c

    mark_v_evac = []
    for s in range(NSUB):
        bank0, bank1 = pb[2], pb[3]
        if s >= 1:
            nc.tensor.wait_ge(aV.h, mark_v_evac[s - 1])
        for k in range(8):
            lhsTh = b["xn_hi"].ap()[:, k, 128 * s:128 * (s + 1)]
            lhsTl = b["xn_lo"].ap()[:, k, 128 * s:128 * (s + 1)]
            for half, bank in ((0, bank0), (1, bank1)):
                rh = b["vw_hi"].ap()[:, k, 512 * half:512 * (half + 1)]
                rl = b["vw_lo"].ap()[:, k, 512 * half:512 * (half + 1)]
                nc.tensor.matmul(out=bank.ap()[:], lhsT=lhsTh, rhs=rh,
                                 start=(k == 0), stop=False)
                nc.tensor.matmul(out=bank.ap()[:], lhsT=lhsTl, rhs=rh,
                                 start=False, stop=False)
                mm = nc.tensor.matmul(out=bank.ap()[:], lhsT=lhsTh, rhs=rl,
                                      start=False, stop=(k == 7))
        mm.then_inc(aT.h, 1)
        aT.inc()
        nc.vector.wait_ge(aT.h, aT.c)
        slot = s + 1
        _hilo(nc.vector, vbh[:, slot, 0:512], vbl[:, slot, 0:512], bank0.ap()[:])
        _hilo(nc.vector, vbh[:, slot, 512:1024], vbl[:, slot, 512:1024],
              bank1.ap()[:]).then_inc(aV.h, 1)
        aV.inc()
        mark_v_evac.append(aV.c)
    nc.vector.drain()
    nc.vector.reg_mov(r_v1, it)
    with nc.vector.If_eq(r_v1, 0):
        nc.vector.tensor_copy(out=b["vg_hi"].ap()[:, :], in_=vbh[0:4, 1, :])
        nc.vector.tensor_copy(out=b["vg_lo"].ap()[:, :], in_=vbl[0:4, 1, :])
    nc.vector.drain()
    nc.vector.sem_inc(aV.h, 1)
    aV.inc()
    mark_v_ready = aV.c

    heads = [(s, h) for s in range(NSUB) for h in range(NHEAD)]
    n_heads = len(heads)
    mark_sc = {}
    mark_tr = {}
    mark_pv = {}
    mark_mask = {}
    mark_p = {}
    mark_exp = {}
    mark_pte = {}
    mark_aoev = {}

    def emit_scores(n):
        s, h = heads[n]
        bank = pb[4 + (n % 2)]
        c = h // 2
        rb_ = 64 * (h % 2)
        if n == 0:
            nc.tensor.wait_ge(aV.h, mark_v_ready)
        if n >= 2:
            nc.tensor.wait_ge(aV.h, mark_mask[n - 2])
        qh = b["qp_hi"].ap()[rb_:rb_ + 64, c, 128 * s:128 * (s + 1)]
        ql = b["qp_lo"].ap()[rb_:rb_ + 64, c, 128 * s:128 * (s + 1)]
        kh_m = kbh[rb_:rb_ + 64, c, 128 * s:128 * s + 256]
        kl_m = kbl[rb_:rb_ + 64, c, 128 * s:128 * s + 256]
        kh_g = kbh[rb_:rb_ + 64, c, 640:644]
        kl_g = kbl[rb_:rb_ + 64, c, 640:644]
        o_m = bank.ap()[:, 0:256]
        o_g = bank.ap()[:, 256:260]
        nc.tensor.matmul(out=o_m, lhsT=qh, rhs=kh_m, start=True, stop=False)
        nc.tensor.matmul(out=o_g, lhsT=qh, rhs=kh_g, start=False, stop=False)
        nc.tensor.matmul(out=o_m, lhsT=qh, rhs=kl_m, start=False, stop=False)
        nc.tensor.matmul(out=o_g, lhsT=qh, rhs=kl_g, start=False, stop=False)
        nc.tensor.matmul(out=o_m, lhsT=ql, rhs=kh_m, start=False, stop=False)
        nc.tensor.matmul(out=o_g, lhsT=ql, rhs=kh_g, start=False,
                         stop=True).then_inc(aT.h, 1)
        aT.inc()
        mark_sc[n] = aT.c

    def emit_softmax_v(n):
        s, h = heads[n]
        par = n % 2
        bank = pb[4 + par]
        nc.vector.wait_ge(aT.h, mark_sc[n])
        nc.vector.tensor_add(out=b["s32"].ap()[:, par, :], in0=bank.ap()[:, 0:260],
                             in1=b["mask_sb"].ap()[:, s, :])
        nc.vector.drain()
        nc.vector.reduce_max(out=b["nmax"].ap()[:, par:par + 1],
                             in_=b["s32"].ap()[:, par, :], axis=AX.X,
                             negate=True).then_inc(aV.h, 1)
        aV.inc()
        mark_mask[n] = aV.c

    def emit_exp(n):
        par = n % 2
        nc.scalar.wait_ge(aV.h, mark_mask[n])
        nc.scalar.activation(out=b["p32"].ap()[:, par, :], in_=b["s32"].ap()[:, par, :],
                             func=AF.Exp, bias=b["nmax"].ap()[:, par:par + 1],
                             accum_out=b["sums"].ap()[:, par:par + 1]).then_inc(aSc.h, 1)
        aSc.inc()
        mark_exp[n] = aSc.c

    def emit_p(n):
        par = n % 2
        nc.vector.wait_ge(aSc.h, mark_exp[n])
        if n >= 2:
            nc.vector.wait_ge(aT.h, mark_tr[n - 2])
        nc.vector.reciprocal(out=b["rs"].ap()[:, par:par + 1],
                             in_=b["sums"].ap()[:, par:par + 1])
        nc.vector.drain()
        nc.vector.tensor_scalar(out=b["p_hi"].ap()[:, par, :],
                                in0=b["p32"].ap()[:, par, :],
                                scalar1=b["rs"].ap()[:, par:par + 1], scalar2=None,
                                op0=OP.mult)
        nc.vector.drain()
        nc.vector.scalar_tensor_tensor(out=b["p_lo"].ap()[:, par, :],
                                       in0=b["p32"].ap()[:, par, :],
                                       scalar=b["rs"].ap()[:, par:par + 1],
                                       in1=b["p_hi"].ap()[:, par, :],
                                       op0=OP.mult, op1=OP.subtract).then_inc(aV.h, 1)
        aV.inc()
        mark_p[n] = aV.c

    def emit_transposes(n):
        par = n % 2
        tp = pt6 if par == 0 else pt7
        nc.tensor.wait_ge(aV.h, mark_p[n])
        if n >= 2:
            nc.tensor.wait_ge(aSc.h, mark_pte[n - 2])
        idn = b["ident"].ap()[:]
        ph = b["p_hi"].ap()[:, par, :]
        pl = b["p_lo"].ap()[:, par, :]
        nc.tensor.transpose(out=tp.ap()[:, 0:128], in_=ph[:, 0:128], identity=idn)
        nc.tensor.transpose(out=tp.ap()[:, 128:256], in_=ph[:, 128:256], identity=idn)
        nc.tensor.transpose(out=tp.ap()[0:4, 256:384], in_=ph[:, 256:260], identity=idn)
        nc.tensor.transpose(out=tp.ap()[:, 384:512], in_=pl[:, 0:128], identity=idn)
        nc.tensor.transpose(out=tp.ap()[:, 512:640], in_=pl[:, 128:256], identity=idn)
        nc.tensor.transpose(out=tp.ap()[0:4, 640:768], in_=pl[:, 256:260],
                            identity=idn).then_inc(aT.h, 1)
        aT.inc()
        mark_tr[n] = aT.c

    def emit_pt_evac(n):
        par = n % 2
        tp = pt6 if par == 0 else pt7
        nc.scalar.wait_ge(aT.h, mark_tr[n])
        if n >= 2:
            nc.scalar.wait_ge(aT.h, mark_pv[n - 2])
        nc.scalar.copy(out=b["pt_hi"].ap()[:, par, 0:256], in_=tp.ap()[:, 0:256])
        nc.scalar.copy(out=b["pt_hi"].ap()[0:4, par, 256:384], in_=tp.ap()[0:4, 256:384])
        nc.scalar.copy(out=b["pt_lo"].ap()[:, par, 0:256], in_=tp.ap()[:, 384:640])
        nc.scalar.copy(out=b["pt_lo"].ap()[0:4, par, 256:384],
                       in_=tp.ap()[0:4, 640:768]).then_inc(aSc.h, 1)
        aSc.inc()
        mark_pte[n] = aSc.c

    def emit_pv(n):
        s, h = heads[n]
        par = n % 2
        pair = 8 * s + h // 2
        bank = pb[pair % 2]
        colbase = ((pair // 2) % 2) * 128
        rb_ = 64 * (h % 2)
        first = (h % 2 == 0)
        last = (h % 2 == 1)
        nc.tensor.wait_ge(aSc.h, mark_pte[n])
        if first and pair >= 2:
            nc.tensor.wait_ge(aV.h, mark_aoev[pair - 2])
        ph = b["pt_hi"].ap()[:, par, :]
        pl = b["pt_lo"].ap()[:, par, :]
        out_ap = bank.ap()[rb_:rb_ + 64, colbase:colbase + 128]
        for blk in range(2):
            vslot = s + blk
            vh = vbh[:, vslot, 64 * h:64 * h + 64]
            vl = vbl[:, vslot, 64 * h:64 * h + 64]
            rh = ph[:, 128 * blk:128 * (blk + 1)]
            rl = pl[:, 128 * blk:128 * (blk + 1)]
            nc.tensor.matmul(out=out_ap, lhsT=vh, rhs=rh, start=(blk == 0), stop=False)
            nc.tensor.matmul(out=out_ap, lhsT=vh, rhs=rl, start=False, stop=False)
            nc.tensor.matmul(out=out_ap, lhsT=vl, rhs=rh, start=False, stop=False)
        gh = b["vg_hi"].ap()[:, 64 * h:64 * h + 64]
        gl = b["vg_lo"].ap()[:, 64 * h:64 * h + 64]
        grh = ph[0:4, 256:384]
        grl = pl[0:4, 256:384]
        nc.tensor.matmul(out=out_ap, lhsT=gh, rhs=grh, start=False, stop=False)
        nc.tensor.matmul(out=out_ap, lhsT=gh, rhs=grl, start=False, stop=False)
        nc.tensor.matmul(out=out_ap, lhsT=gl, rhs=grh, start=False,
                         stop=True).then_inc(aT.h, 1)
        aT.inc()
        mark_pv[n] = aT.c
        if last:
            a_in_s = pair % 8
            nc.vector.wait_ge(aT.h, mark_pv[n])
            _hilo(nc.vector, b["ao_hi"].ap()[:, a_in_s, 128 * s:128 * (s + 1)],
                  b["ao_lo"].ap()[:, a_in_s, 128 * s:128 * (s + 1)],
                  bank.ap()[:, colbase:colbase + 128]).then_inc(aV.h, 1)
            aV.inc()
            mark_aoev[pair] = aV.c

    emit_scores(0)
    for n in range(n_heads):
        emit_softmax_v(n)
        emit_exp(n)
        emit_p(n)
        if n + 1 < n_heads:
            emit_scores(n + 1)
        emit_transposes(n)
        emit_pt_evac(n)
        emit_pv(n)
    mark_all_ao = aV.c

    oh_r = dr["outh_d"].ap().rearrange("k p j -> p k j")
    ol_r = dr["outl_d"].ap().rearrange("k p j -> p k j")
    mark_op = []
    mark_res = []
    for m in range(8):
        if m >= 2:
            nc.sync.wait_ge(aT.h, mark_op[m - 2])
        dma(nc.sync, aOW[m % 2], b["ow_hi"].ap()[:, m % 2],
            oh_r[:, :, 128 * m:128 * (m + 1)])
        dma(nc.sync, aOW[m % 2], b["ow_lo"].ap()[:, m % 2],
            ol_r[:, :, 128 * m:128 * (m + 1)])
        bank = pb[2 + m % 2]
        nc.tensor.wait_ge(aOW[m % 2].h, 32 * (m // 2 + 1))
        if m == 0:
            nc.tensor.wait_ge(aV.h, mark_all_ao)
        if m >= 2:
            nc.tensor.wait_ge(aV.h, mark_res[m - 2])
        for k in range(8):
            lh = b["ow_hi"].ap()[:, m % 2, k, :]
            ll = b["ow_lo"].ap()[:, m % 2, k, :]
            rh = b["ao_hi"].ap()[:, k, :]
            rl = b["ao_lo"].ap()[:, k, :]
            nc.tensor.matmul(out=bank.ap()[:], lhsT=lh, rhs=rh, start=(k == 0),
                             stop=False)
            nc.tensor.matmul(out=bank.ap()[:], lhsT=lh, rhs=rl, start=False, stop=False)
            mm = nc.tensor.matmul(out=bank.ap()[:], lhsT=ll, rhs=rh, start=False,
                                  stop=(k == 7))
        mm.then_inc(aT.h, 1)
        aT.inc()
        mark_op.append(aT.c)
        nc.vector.wait_ge(aT.h, aT.c)
        nc.vector.tensor_add(out=b["x_sb"].ap()[:, m, :], in0=b["x_sb"].ap()[:, m, :],
                             in1=bank.ap()[:]).then_inc(aV.h, 1)
        aV.inc()
        mark_res.append(aV.c)

    for c in range(8):
        nc.vector.tensor_copy(out=kbh[:, c, 0:128], in_=kbh[:, c, 512:640])
        nc.vector.tensor_copy(out=kbl[:, c, 0:128], in_=kbl[:, c, 512:640])
    nc.vector.tensor_copy(out=vbh[:, 0, :], in_=vbh[:, 4, :])
    nc.vector.tensor_copy(out=vbl[:, 0, :], in_=vbl[:, 4, :]).then_inc(aV.h, 1)
    aV.inc()
    nc.sync.wait_ge(aV.h, aV.c)
    nc.sync.reg_mov(r_s1, it)
    nc.sync.reg_mul(r_s1, r_s1, T)
    dma(nc.sync, aSP, bass.AP(dr["x1s_d"], r_s1, [[S, 128], [128 * S, 8], [1, T]]),
        b["x_sb"].ap())

    loop_seal(loop_sems,
              [(aX, aX.c), (aM, aM.c), (aQW[0], aQW[0].c), (aQW[1], aQW[1].c),
               (aOW[0], aOW[0].c), (aOW[1], aOW[1].c), (aSP, aSP.c)])



def _emit_ffn_tile(nc, it, dma, loop_seal, loop_sems, sm, dr, b, pb, regs,
                   mono=None, aDone=None, in_loop=True):
    fX, fW, fT, fV, fS, fO = sm["fX"], sm["fW"], sm["fT"], sm["fV"], sm["fS"], sm["fO"]
    r_s1, r_s2, r_v1 = regs
    for s in (fX, fT, fV, fS, fO[0], fO[1], fW[0], fW[1]):
        s.reset()
    nc.gpsimd.wait_ge(aDone.h, 1)

    bases = mono["bases"]
    tmps = mono["tmps"]
    NAME = {id(fX): "fX", id(fT): "fT", id(fV): "fV", id(fS): "fS",
            id(fO[0]): "fO0", id(fO[1]): "fO1", id(fW[0]): "fW0",
            id(fW[1]): "fW1"}
    EKEY = {id(nc.sync): "sync", id(nc.tensor): "tensor", id(nc.vector): "vector",
            id(nc.scalar): "scalar", id(nc.gpsimd): "gpsimd"}

    def w8(eng, sem, cval):
        ek = EKEY[id(eng)]
        base = bases[(ek, NAME[id(sem)])]
        tmp = tmps[ek]
        eng.reg_mov(tmp, base)
        if cval:
            eng.reg_add(tmp, tmp, cval)
        eng.wait_ge(sem.h, tmp)

    nc.sync.reg_mov(r_s1, it)
    nc.sync.reg_mul(r_s1, r_s1, T)
    dma(nc.sync, fX, b["fx_sb"].ap(),
        bass.AP(dr["x1s_d"], r_s1, [[S, 128], [128 * S, 8], [1, T]]))

    ln_scr = (b["fln_m"].ap()[:], b["fln_sq"].ap()[:], b["fln_var"].ap()[:],
              b["fln_sd"].ap()[:], b["fln_a"].ap()[:], b["fln_b"].ap()[:],
              b["fact"].ap()[:, 0, :], b["fact"].ap()[:, 1, :])
    _emit_ln(nc, b["fx_sb"], b["fxn_hi"], b["fxn_lo"], ln_scr,
             (b["ones_c"].ap()[:], b["ones_r"].ap()[:], b["eps_sb"].ap()[:]),
             (pb[4], pb[5], pb[2], pb[3]), fT, fV, fS, fX, 16, w8=w8)
    mark_ln = fV.c

    wf = b["wf"].ap()
    state = {"in_v": mark_ln, "in_s": 0}

    def stage(name, nm, kc, src_hi, src_lo, tapes, emit_epilogue):
        stage_fT0 = fT.c
        mark_ep = []
        for m in range(nm):
            if m >= 2:
                w8(nc.gpsimd, fT, stage_fT0 + m - 1)
            else:
                w8(nc.gpsimd, fT, stage_fT0)
            for (ap_fn, off, cols) in tapes:
                dma(nc.gpsimd, fW[m % 2], wf[:, m % 2, off:off + cols], ap_fn(m))
            w8(nc.tensor, fW[m % 2], fW[m % 2].c)
            if m == 0:
                w8(nc.tensor, fV, state["in_v"])
                if state["in_s"]:
                    w8(nc.tensor, fS, state["in_s"])
            if m >= 2:
                w8(nc.tensor, fV, mark_ep[m - 2])
            if name == "s1":
                bankA, bankB = pb[2 * (m % 2)], pb[2 * (m % 2) + 1]
                for k in range(kc):
                    lw1 = wf[:, m % 2, 128 * k:128 * (k + 1)]
                    lw2 = wf[:, m % 2, 1024 + 128 * k:1024 + 128 * (k + 1)]
                    xh = src_hi[:, k, :]
                    xl = src_lo[:, k, :]
                    nc.tensor.matmul(out=bankA.ap()[:], lhsT=lw1, rhs=xh,
                                     start=(k == 0), stop=False)
                    nc.tensor.matmul(out=bankA.ap()[:], lhsT=lw1, rhs=xl,
                                     start=False, stop=(k == kc - 1))
                    nc.tensor.matmul(out=bankB.ap()[:], lhsT=lw2, rhs=xh,
                                     start=(k == 0), stop=False)
                    mm = nc.tensor.matmul(out=bankB.ap()[:], lhsT=lw2, rhs=xl,
                                          start=False, stop=(k == kc - 1))
                mm.then_inc(fT.h, 1)
                fT.inc()
                emit_epilogue(m, (bankA, bankB), mark_ep)
            else:
                bank = pb[m % 2]
                for k in range(kc):
                    lw = wf[:, m % 2, 128 * k:128 * (k + 1)]
                    xh = src_hi[:, k, :]
                    xl = src_lo[:, k, :]
                    nc.tensor.matmul(out=bank.ap()[:], lhsT=lw, rhs=xh,
                                     start=(k == 0), stop=False)
                    mm = nc.tensor.matmul(out=bank.ap()[:], lhsT=lw, rhs=xl,
                                          start=False, stop=(k == kc - 1))
                mm.then_inc(fT.h, 1)
                fT.inc()
                emit_epilogue(m, bank, mark_ep)
            mark_ep.append(fV.c)
        state["in_v"] = fV.c
        state["in_s"] = fS.c

    def epi1(m, banks, mark_ep):
        bankA, bankB = banks
        w8(nc.scalar, fT, fT.c)
        if m >= 2:
            w8(nc.scalar, fV, mark_ep[m - 2])
        nc.scalar.activation(out=b["fact"].ap()[:, m % 2, :], in_=bankA.ap()[:],
                             func=AF.Silu, scale=b["sc1"].ap()[:, m:m + 1]
                             ).then_inc(fS.h, 1)
        fS.inc()
        w8(nc.vector, fS, fS.c)
        nc.vector.scalar_tensor_tensor(out=b["fh32"].ap()[:, m % 2, :],
                                       in0=bankB.ap()[:],
                                       scalar=b["sc2"].ap()[:, m:m + 1],
                                       in1=b["fact"].ap()[:, m % 2, :],
                                       op0=OP.mult, op1=OP.mult)
        _hilo(nc.vector, b["h_hi"].ap()[:, m, :], b["h_lo"].ap()[:, m, :],
              b["fh32"].ap()[:, m % 2, :]).then_inc(fV.h, 1)
        fV.inc()

    def epi_gelu(dst_hi, dst_lo, ssc):
        def _e(m, bank, mark_ep):
            w8(nc.scalar, fT, fT.c)
            if m >= 2:
                w8(nc.scalar, fV, mark_ep[m - 2])
            nc.scalar.activation(out=b["fact"].ap()[:, m % 2, :], in_=bank.ap()[:],
                                 func=AF.Gelu, scale=ssc[:, m:m + 1]).then_inc(fS.h, 1)
            fS.inc()
            w8(nc.vector, fS, fS.c)
            _hilo(nc.vector, dst_hi[:, m, :], dst_lo[:, m, :],
                  b["fact"].ap()[:, m % 2, :]).then_inc(fV.h, 1)
            fV.inc()
        return _e

    def epi4(m, bank, mark_ep):
        w8(nc.vector, fT, fT.c)
        if m >= 2:
            w8(nc.vector, fO[m % 2], 16 * (m // 2))
        nc.vector.scalar_tensor_tensor(out=b["fo32"].ap()[:, m % 2, :], in0=bank.ap()[:],
                                       scalar=b["sc3"].ap()[:, m:m + 1],
                                       in1=b["fx_sb"].ap()[:, m, :],
                                       op0=OP.mult, op1=OP.add).then_inc(fV.h, 1)
        fV.inc()
        w8(nc.sync, fV, fV.c)
        nc.sync.reg_mov(r_s2, it)
        nc.sync.reg_mul(r_s2, r_s2, T)
        nc.sync.reg_add(r_s2, r_s2, m * 128 * S)
        dma(nc.sync, fO[m % 2], bass.AP(dr["y_d"], r_s2, [[S, 128], [1, T]]),
            b["fo32"].ap()[:, m % 2, :])

    n_stages = int(os.environ.get("MK_STAGES", "4"))
    if n_stages >= 1:
        stage("s1", 32, 8, b["fxn_hi"].ap(), b["fxn_lo"].ap(),
              [(lambda m: dr["tw1_d"].ap()[m], 0, 1024),
               (lambda m: dr["tw2_d"].ap()[m], 1024, 1024)], epi1)
    if n_stages >= 2:
        stage("s2", 32, 32, b["h_hi"].ap(), b["h_lo"].ap(),
              [(lambda m: dr["th1_d"].ap()[m], 0, 4096)],
              epi_gelu(b["g_hi"].ap(), b["g_lo"].ap(), b["sh1"].ap()))
    if n_stages >= 3:
        stage("s3", 32, 32, b["g_hi"].ap(), b["g_lo"].ap(),
              [(lambda m: dr["th2_d"].ap()[m], 0, 4096)],
              epi_gelu(b["h_hi"].ap(), b["h_lo"].ap(), b["sh2"].ap()))
    if n_stages >= 4:
        stage("s4", 8, 32, b["h_hi"].ap(), b["h_lo"].ap(),
              [(lambda m: dr["tw3_d"].ap()[m], 0, 4096)], epi4)
    else:
        nc.sync.wait_ge(fV.h, fV.c)
        nc.sync.reg_mov(r_s2, it)
        nc.sync.reg_mul(r_s2, r_s2, T)
        dma(nc.sync, fO[0], bass.AP(dr["y_d"], r_s2, [[S, 128], [128 * S, 8], [1, T]]),
            b["fx_sb"].ap())

    w8(nc.sync, fO[0], fO[0].c)
    w8(nc.sync, fO[1], fO[1].c)
    DELTA = {"fX": fX.c, "fT": fT.c, "fV": fV.c, "fS": fS.c, "fO0": fO[0].c,
             "fO1": fO[1].c, "fW0": fW[0].c, "fW1": fW[1].c}
    ENG = {"sync": nc.sync, "tensor": nc.tensor, "vector": nc.vector,
           "scalar": nc.scalar, "gpsimd": nc.gpsimd}
    for (ek, sname), reg in bases.items():
        ENG[ek].reg_add(reg, reg, DELTA[sname])
    mono["delta"] = DELTA
    nc.multi_engine_barrier([mybir.EngineType.SP, mybir.EngineType.PE,
                             mybir.EngineType.DVE, mybir.EngineType.Activation])



def build_nc():
    nc = bass.Bass("TRN2", target_bir_lowering=False)

    x_d = nc.dram_tensor("x", [D, S], F32, kind="ExternalInput")
    qkvT_d = nc.dram_tensor("qkvT", [D, 3 * D], F32, kind="ExternalInput")
    outT_d = nc.dram_tensor("outT", [D, D], F32, kind="ExternalInput")
    tw1_d = nc.dram_tensor("tw1", [32, 128, 1024], F32, kind="ExternalInput")
    tw2_d = nc.dram_tensor("tw2", [32, 128, 1024], F32, kind="ExternalInput")
    th1_d = nc.dram_tensor("th1", [32, 128, 4096], F32, kind="ExternalInput")
    th2_d = nc.dram_tensor("th2", [32, 128, 4096], F32, kind="ExternalInput")
    tw3_d = nc.dram_tensor("tw3", [8, 128, 4096], F32, kind="ExternalInput")
    sc1_d = nc.dram_tensor("sc1", [128, 32], F32, kind="ExternalInput")
    sc2_d = nc.dram_tensor("sc2", [128, 32], F32, kind="ExternalInput")
    sh1_d = nc.dram_tensor("sh1", [128, 32], F32, kind="ExternalInput")
    sh2_d = nc.dram_tensor("sh2", [128, 32], F32, kind="ExternalInput")
    sc3_d = nc.dram_tensor("sc3", [128, 8], F32, kind="ExternalInput")
    cos_d = nc.dram_tensor("cosT", [128, S], F32, kind="ExternalInput")
    sin_d = nc.dram_tensor("sinT", [128, S], F32, kind="ExternalInput")
    mask_d = nc.dram_tensor("maskT", [16, 128, 260], F32, kind="ExternalInput")
    ident_d = nc.dram_tensor("ident", [128, 128], F32, kind="ExternalInput")
    ones_c_d = nc.dram_tensor("ones_c", [128, 1], F32, kind="ExternalInput")
    ones_r_d = nc.dram_tensor("ones_r", [1, 128], F32, kind="ExternalInput")
    eps_d = nc.dram_tensor("eps_c", [1, 1], F32, kind="ExternalInput")
    zeros_d = nc.dram_tensor("zeros", [128, 5504], F32, kind="ExternalInput")
    y_d = nc.dram_tensor("y", [D, S], F32, kind="ExternalOutput")

    qkvh_d = nc.dram_tensor("qkv_hi", [8, 128, 3 * D], BF16, kind="Internal")
    qkvl_d = nc.dram_tensor("qkv_lo", [8, 128, 3 * D], BF16, kind="Internal")
    outh_d = nc.dram_tensor("out_hi", [8, 128, D], BF16, kind="Internal")
    outl_d = nc.dram_tensor("out_lo", [8, 128, D], BF16, kind="Internal")
    x1s_d = nc.dram_tensor("x1s", [D, S], F32, kind="Internal")

    es = contextlib.ExitStack()
    with es:
        pb = [es.enter_context(nc.psum_tensor(f"pb{j}", [128, 512], F32))
              for j in range(6)]
        pt6 = es.enter_context(nc.psum_tensor("pt6", [128, 1024], BF16))
        pt7 = es.enter_context(nc.psum_tensor("pt7", [128, 1024], BF16))

        pL = Sem(nc, es, "pL")
        pV = Sem(nc, es, "pV")
        pS = Sem(nc, es, "pS")
        pZ = Sem(nc, es, "pZ")
        aX = Sem(nc, es, "aX")
        aM = Sem(nc, es, "aM")
        aQW = (Sem(nc, es, "aQW0"), Sem(nc, es, "aQW1"))
        aOW = (Sem(nc, es, "aOW0"), Sem(nc, es, "aOW1"))
        aT = Sem(nc, es, "aT")
        aV = Sem(nc, es, "aV")
        aSc = Sem(nc, es, "aSc")
        aSP = Sem(nc, es, "aSP")
        aDone = Sem(nc, es, "aDone")
        fTm = Sem(nc, es, "fTm")
        LOOP_A = [aX, aM, aQW[0], aQW[1], aOW[0], aOW[1], aT, aV, aSc, aSP]
        fX = Sem(nc, es, "fX")
        fW = (Sem(nc, es, "fW0"), Sem(nc, es, "fW1"))
        fT = Sem(nc, es, "fT")
        fV = Sem(nc, es, "fV")
        fS = Sem(nc, es, "fS")
        fO = (Sem(nc, es, "fO0"), Sem(nc, es, "fO1"))
        LOOP_F = [fX, fT, fV, fS, fO[0], fO[1]]

        ident = es.enter_context(nc.sbuf_tensor("identC", [128, 128], BF16))
        ident32 = es.enter_context(nc.sbuf_tensor("identC32", [128, 128], F32))
        ones_c = es.enter_context(nc.sbuf_tensor("onesC", [128, 1], F32))
        ones_r = es.enter_context(nc.sbuf_tensor("onesR", [1, 128], F32))
        eps_sb = es.enter_context(nc.sbuf_tensor("epsC", [1, 1], F32))

        r_s1 = nc.sync.alloc_register("r_s1")
        r_s2 = nc.sync.alloc_register("r_s2")
        r_v1 = nc.vector.alloc_register("r_v1")
        regs = (r_s1, r_s2, r_v1)

        def dma(eng, sem, out, in_):
            eng.dma_start(out=out, in_=in_).then_inc(sem.h, 16)
            sem.inc(16)

        ENG4 = [mybir.EngineType.SP, mybir.EngineType.PE, mybir.EngineType.DVE,
                mybir.EngineType.Activation]

        def barrier4():
            nc.multi_engine_barrier(ENG4)

        def loop_seal(sems, dma_waits):
            for sem, cnt in dma_waits:
                nc.sync.wait_ge(sem.h, cnt)
            nc.all_engine_barrier()
            for sem in sems:
                nc.sync.sem_clear(sem.h)
                sem.reset()
            nc.all_engine_barrier()

        dma(nc.sync, pL, ident32[:], ident_d.ap())
        dma(nc.sync, pL, ones_c[:], ones_c_d.ap())
        dma(nc.sync, pL, ones_r[:], ones_r_d.ap())
        dma(nc.sync, pL, eps_sb[:], eps_d.ap())

        at_es = contextlib.ExitStack()
        x_sb = at_es.enter_context(nc.sbuf_tensor("x_sb", [128, 8, T], F32))
        xn_hi = at_es.enter_context(nc.sbuf_tensor("xn_hi", [128, 8, T], BF16))
        xn_lo = at_es.enter_context(nc.sbuf_tensor("xn_lo", [128, 8, T], BF16))
        qp_hi = at_es.enter_context(nc.sbuf_tensor("qp_hi", [128, 8, T], BF16))
        qp_lo = at_es.enter_context(nc.sbuf_tensor("qp_lo", [128, 8, T], BF16))
        kb_hi = at_es.enter_context(nc.sbuf_tensor("kb_hi", [128, 8, 644], BF16))
        kb_lo = at_es.enter_context(nc.sbuf_tensor("kb_lo", [128, 8, 644], BF16))
        vb_hi = at_es.enter_context(nc.sbuf_tensor("vb_hi", [128, 5, 1024], BF16))
        vb_lo = at_es.enter_context(nc.sbuf_tensor("vb_lo", [128, 5, 1024], BF16))
        vg_hi = at_es.enter_context(nc.sbuf_tensor("vg_hi", [4, 1024], BF16))
        vg_lo = at_es.enter_context(nc.sbuf_tensor("vg_lo", [4, 1024], BF16))
        ao_hi = at_es.enter_context(nc.sbuf_tensor("ao_hi", [128, 8, T], BF16))
        ao_lo = at_es.enter_context(nc.sbuf_tensor("ao_lo", [128, 8, T], BF16))
        vw_hi = at_es.enter_context(nc.sbuf_tensor("vw_hi", [128, 8, 1024], BF16))
        vw_lo = at_es.enter_context(nc.sbuf_tensor("vw_lo", [128, 8, 1024], BF16))
        qkw_hi = at_es.enter_context(nc.sbuf_tensor("qkw_hi", [128, 2, 8, 256], BF16))
        qkw_lo = at_es.enter_context(nc.sbuf_tensor("qkw_lo", [128, 2, 8, 256], BF16))
        ow_hi = at_es.enter_context(nc.sbuf_tensor("ow_hi", [128, 2, 8, 128], BF16))
        ow_lo = at_es.enter_context(nc.sbuf_tensor("ow_lo", [128, 2, 8, 128], BF16))
        ropeC = at_es.enter_context(nc.sbuf_tensor("ropeC", [128, T], F32))
        ropeS = at_es.enter_context(nc.sbuf_tensor("ropeS", [128, T], F32))
        mask_sb = at_es.enter_context(nc.sbuf_tensor("mask_sb", [128, 4, 260], F32))
        sc_q32 = at_es.enter_context(nc.sbuf_tensor("sc_q32", [128, T], F32))
        sc_qs = at_es.enter_context(nc.sbuf_tensor("sc_qs", [128, T], F32))
        sc_u = at_es.enter_context(nc.sbuf_tensor("sc_u", [128, T], F32))
        sc_v = at_es.enter_context(nc.sbuf_tensor("sc_v", [128, T], F32))
        s32 = at_es.enter_context(nc.sbuf_tensor("s32", [128, 2, 260], F32))
        p32 = at_es.enter_context(nc.sbuf_tensor("p32", [128, 2, 260], F32))
        p_hi = at_es.enter_context(nc.sbuf_tensor("p_hi", [128, 2, 260], BF16))
        p_lo = at_es.enter_context(nc.sbuf_tensor("p_lo", [128, 2, 260], BF16))
        pt_hi = at_es.enter_context(nc.sbuf_tensor("pt_hi", [128, 2, 384], BF16))
        pt_lo = at_es.enter_context(nc.sbuf_tensor("pt_lo", [128, 2, 384], BF16))
        nmax = at_es.enter_context(nc.sbuf_tensor("nmax", [128, 2], F32))
        sums = at_es.enter_context(nc.sbuf_tensor("sums", [128, 2], F32))
        rs = at_es.enter_context(nc.sbuf_tensor("rs", [128, 2], F32))
        ln_m = at_es.enter_context(nc.sbuf_tensor("ln_m", [1, T], F32))
        ln_sq = at_es.enter_context(nc.sbuf_tensor("ln_sq", [1, T], F32))
        ln_var = at_es.enter_context(nc.sbuf_tensor("ln_var", [1, T], F32))
        ln_sd = at_es.enter_context(nc.sbuf_tensor("ln_sd", [1, T], F32))
        ln_a = at_es.enter_context(nc.sbuf_tensor("ln_a", [1, T], F32))
        ln_b = at_es.enter_context(nc.sbuf_tensor("ln_b", [1, T], F32))

        with at_es:
            xflat = x_sb.ap().rearrange("p a b -> p (a b)")
            aoh_f = ao_hi.ap().rearrange("p a b -> p (a b)")
            aol_f = ao_lo.ap().rearrange("p a b -> p (a b)")
            qkvT = qkvT_d.ap()
            outT = outT_d.ap()

            nc.vector.wait_ge(pL.h, pL.c)
            nc.vector.tensor_copy(out=ident[:], in_=ident32[:]).then_inc(pV.h, 1)
            pV.inc()
            pV_base = pV.c
            pS_base = pS.c
            for k in range(8):
                nc.sync.wait_ge(pV.h, pV_base + k)
                dma(nc.sync, pL, xflat[:, 0:3 * D], qkvT[k * 128:(k + 1) * 128, :])
                nc.vector.wait_ge(pL.h, pL.c)
                if k > 0:
                    nc.vector.wait_ge(pS.h, pS_base + 32 * k)
                _hilo(nc.vector, aoh_f[:, 0:D], aol_f[:, 0:D], xflat[:, 0:D],
                      scale=0.125)
                _hilo(nc.vector, aoh_f[:, D:3 * D], aol_f[:, D:3 * D],
                      xflat[:, D:3 * D]).then_inc(pV.h, 1)
                pV.inc()
                nc.sync.wait_ge(pV.h, pV.c)
                dma(nc.sync, pS, qkvh_d.ap()[k], aoh_f[:, 0:3 * D])
                dma(nc.sync, pS, qkvl_d.ap()[k], aol_f[:, 0:3 * D])
            for k in range(8):
                nc.sync.wait_ge(pV.h, pV_base + 8 + k)
                dma(nc.sync, pL, xflat[:, 0:D], outT[k * 128:(k + 1) * 128, :])
                nc.vector.wait_ge(pL.h, pL.c)
                nc.vector.wait_ge(pS.h, pS_base + 256 + 32 * k)
                _hilo(nc.vector, aoh_f[:, 0:D], aol_f[:, 0:D],
                      xflat[:, 0:D]).then_inc(pV.h, 1)
                pV.inc()
                nc.sync.wait_ge(pV.h, pV.c)
                dma(nc.sync, pS, outh_d.ap()[k], aoh_f[:, 0:D])
                dma(nc.sync, pS, outl_d.ap()[k], aol_f[:, 0:D])

            nc.sync.wait_ge(pS.h, pS.c)
            qh_r = qkvh_d.ap().rearrange("k p j -> p k j")
            ql_r = qkvl_d.ap().rearrange("k p j -> p k j")
            dma(nc.sync, pL, vw_hi[:], qh_r[:, :, 2 * D:3 * D])
            dma(nc.sync, pL, vw_lo[:], ql_r[:, :, 2 * D:3 * D])
            zr = zeros_d.ap()
            kbh_f = kb_hi.ap().rearrange("p a b -> p (a b)")
            kbl_f = kb_lo.ap().rearrange("p a b -> p (a b)")
            vbh_f = vb_hi.ap().rearrange("p a b -> p (a b)")
            vbl_f = vb_lo.ap().rearrange("p a b -> p (a b)")
            dma(nc.gpsimd, pZ, kbh_f[:, :], zr[:, 0:8 * 644])
            dma(nc.gpsimd, pZ, kbl_f[:, :], zr[:, 0:8 * 644])
            dma(nc.gpsimd, pZ, vbh_f[:, :], zr[:, 0:5 * 1024])
            dma(nc.gpsimd, pZ, vbl_f[:, :], zr[:, 0:5 * 1024])
            dma(nc.gpsimd, pZ, vg_hi[:], zr[0:4, 0:1024])
            dma(nc.gpsimd, pZ, vg_lo[:], zr[0:4, 0:1024])
            nc.sync.wait_ge(pL.h, pL.c)
            nc.sync.wait_ge(pZ.h, pZ.c)
            nc.gpsimd.wait_ge(pZ.h, pZ.c)
            nc.all_engine_barrier()

            bufs_a = dict(x_sb=x_sb, xn_hi=xn_hi, xn_lo=xn_lo, qp_hi=qp_hi,
                          qp_lo=qp_lo, kb_hi=kb_hi, kb_lo=kb_lo, vb_hi=vb_hi,
                          vb_lo=vb_lo, vg_hi=vg_hi, vg_lo=vg_lo, ao_hi=ao_hi,
                          ao_lo=ao_lo, vw_hi=vw_hi, vw_lo=vw_lo, qkw_hi=qkw_hi,
                          qkw_lo=qkw_lo, ow_hi=ow_hi, ow_lo=ow_lo, ropeC=ropeC,
                          ropeS=ropeS, mask_sb=mask_sb, sc_q32=sc_q32, sc_qs=sc_qs,
                          sc_u=sc_u, sc_v=sc_v, s32=s32, p32=p32, p_hi=p_hi, p_lo=p_lo,
                          pt_hi=pt_hi, pt_lo=pt_lo, nmax=nmax, sums=sums, rs=rs,
                          ln_m=ln_m, ln_sq=ln_sq, ln_var=ln_var, ln_sd=ln_sd,
                          ln_a=ln_a, ln_b=ln_b, ident=ident, ones_c=ones_c,
                          ones_r=ones_r, eps_sb=eps_sb)
            dr_a = dict(x_d=x_d, cos_d=cos_d, sin_d=sin_d, mask_d=mask_d,
                        x1s_d=x1s_d, qh_r=qh_r, ql_r=ql_r, outh_d=outh_d,
                        outl_d=outl_d)
            sm_a = dict(aX=aX, aM=aM, aQW=aQW, aOW=aOW, aT=aT, aV=aV, aSc=aSc,
                        aSP=aSP)
            if BUILD_ATTN:
                with nc.Fori(0, NT) as it:
                    _emit_attention_tile(nc, it, dma, loop_seal, LOOP_A, sm_a, dr_a,
                                         bufs_a, pb, pt6, pt7, regs)
            else:
                with nc.Fori(0, NT) as it:
                    nc.sync.reg_mov(r_s1, it)
                    nc.sync.reg_mul(r_s1, r_s1, T)
                    dma(nc.sync, aX, x_sb.ap(),
                        bass.AP(x_d, r_s1, [[S, 128], [128 * S, 8], [1, T]]))
                    nc.sync.wait_ge(aX.h, aX.c)
                    nc.sync.reg_mov(r_s2, it)
                    nc.sync.reg_mul(r_s2, r_s2, T)
                    dma(nc.sync, aSP,
                        bass.AP(x1s_d, r_s2, [[S, 128], [128 * S, 8], [1, T]]),
                        x_sb.ap())
                    loop_seal(LOOP_A, [(aSP, aSP.c)])

        nc.sync.sem_inc(aDone.h, 1)

        ffn_es = contextlib.ExitStack()
        fx_sb = ffn_es.enter_context(nc.sbuf_tensor("fx_sb", [128, 8, T], F32))
        fxn_hi = ffn_es.enter_context(nc.sbuf_tensor("fxn_hi", [128, 8, T], BF16))
        fxn_lo = ffn_es.enter_context(nc.sbuf_tensor("fxn_lo", [128, 8, T], BF16))
        h_hi = ffn_es.enter_context(nc.sbuf_tensor("h_hi", [128, 32, T], BF16))
        h_lo = ffn_es.enter_context(nc.sbuf_tensor("h_lo", [128, 32, T], BF16))
        g_hi = ffn_es.enter_context(nc.sbuf_tensor("g_hi", [128, 32, T], BF16))
        g_lo = ffn_es.enter_context(nc.sbuf_tensor("g_lo", [128, 32, T], BF16))
        wf = ffn_es.enter_context(nc.sbuf_tensor("wf", [128, 2, 4096], BF16))
        sc1 = ffn_es.enter_context(nc.sbuf_tensor("sc1s", [128, 32], F32))
        sc2 = ffn_es.enter_context(nc.sbuf_tensor("sc2s", [128, 32], F32))
        sh1 = ffn_es.enter_context(nc.sbuf_tensor("sh1s", [128, 32], F32))
        sh2 = ffn_es.enter_context(nc.sbuf_tensor("sh2s", [128, 32], F32))
        sc3 = ffn_es.enter_context(nc.sbuf_tensor("sc3s", [128, 8], F32))
        fact = ffn_es.enter_context(nc.sbuf_tensor("fact", [128, 2, T], F32))
        fh32 = ffn_es.enter_context(nc.sbuf_tensor("fh32", [128, 2, T], F32))
        fo32 = ffn_es.enter_context(nc.sbuf_tensor("fo32", [128, 2, T], F32))
        fln_m = ffn_es.enter_context(nc.sbuf_tensor("fln_m", [1, T], F32))
        fln_sq = ffn_es.enter_context(nc.sbuf_tensor("fln_sq", [1, T], F32))
        fln_var = ffn_es.enter_context(nc.sbuf_tensor("fln_var", [1, T], F32))
        fln_sd = ffn_es.enter_context(nc.sbuf_tensor("fln_sd", [1, T], F32))
        fln_a = ffn_es.enter_context(nc.sbuf_tensor("fln_a", [1, T], F32))
        fln_b = ffn_es.enter_context(nc.sbuf_tensor("fln_b", [1, T], F32))

        with ffn_es:
            bufs_f = dict(fx_sb=fx_sb, fxn_hi=fxn_hi, fxn_lo=fxn_lo, h_hi=h_hi,
                          h_lo=h_lo, g_hi=g_hi, g_lo=g_lo, wf=wf, sc1=sc1, sc2=sc2,
                          sh1=sh1, sh2=sh2, sc3=sc3, fact=fact, fh32=fh32,
                          fo32=fo32, fln_m=fln_m, fln_sq=fln_sq, fln_var=fln_var,
                          fln_sd=fln_sd, fln_a=fln_a, fln_b=fln_b, ones_c=ones_c,
                          ones_r=ones_r, eps_sb=eps_sb)
            dr_f = dict(x1s_d=x1s_d, y_d=y_d, tw1_d=tw1_d, tw2_d=tw2_d, th1_d=th1_d,
                        th2_d=th2_d, tw3_d=tw3_d)
            sm_f = dict(fX=fX, fW=fW, fT=fT, fV=fV, fS=fS, fO=fO)
            NOLOOP = os.environ.get("MK_NOLOOP", "0") == "1"
            if BUILD_FFN:
                dma(nc.sync, pL, sc1[:], sc1_d.ap())
                dma(nc.sync, pL, sc2[:], sc2_d.ap())
                dma(nc.sync, pL, sh1[:], sh1_d.ap())
                dma(nc.sync, pL, sh2[:], sh2_d.ap())
                dma(nc.sync, pL, sc3[:], sc3_d.ap())
                nc.sync.wait_ge(pL.h, pL.c)
                barrier4()
                PAIRS = [("scalar", "fX"), ("scalar", "fT"), ("scalar", "fV"),
                         ("tensor", "fS"), ("tensor", "fV"), ("tensor", "fW0"),
                         ("tensor", "fW1"), ("vector", "fT"), ("vector", "fS"),
                         ("vector", "fO0"), ("vector", "fO1"), ("sync", "fV"),
                         ("sync", "fO0"), ("sync", "fO1"), ("gpsimd", "fT")]
                ENG = {"sync": nc.sync, "tensor": nc.tensor, "vector": nc.vector,
                       "scalar": nc.scalar, "gpsimd": nc.gpsimd}
                bases = {}
                for ek, sname in PAIRS:
                    r = ENG[ek].alloc_register(f"mb_{ek}_{sname}")
                    ENG[ek].reg_mov(r, 0)
                    bases[(ek, sname)] = r
                tmps = {}
                for ek in ENG:
                    tmps[ek] = ENG[ek].alloc_register(f"mt_{ek}")
                mono = dict(bases=bases, tmps=tmps)
                if NOLOOP:
                    for it in range(int(os.environ.get("MK_NTILES", "1"))):
                        _emit_ffn_tile(nc, it, dma, loop_seal, LOOP_F, sm_f, dr_f,
                                       bufs_f, pb, regs, mono=mono,
                                       aDone=aDone, in_loop=False)
                else:
                    with nc.Fori(0, NT) as it:
                        _emit_ffn_tile(nc, it, dma, loop_seal, LOOP_F, sm_f, dr_f,
                                       bufs_f, pb, regs, mono=mono,
                                       aDone=aDone, in_loop=True)
                d = mono["delta"]
                niters = int(os.environ.get("MK_NTILES", "1")) if NOLOOP else NT
                nc.sync.wait_ge(fO[0].h, niters * d["fO0"])
                nc.sync.wait_ge(fO[1].h, niters * d["fO1"])
            else:
                with nc.Fori(0, NT) as it:
                    nc.sync.reg_mov(r_s1, it)
                    nc.sync.reg_mul(r_s1, r_s1, T)
                    dma(nc.sync, fX, fx_sb.ap(),
                        bass.AP(x1s_d, r_s1, [[S, 128], [128 * S, 8], [1, T]]))
                    nc.sync.wait_ge(fX.h, fX.c)
                    nc.sync.reg_mov(r_s2, it)
                    nc.sync.reg_mul(r_s2, r_s2, T)
                    dma(nc.sync, fO[0],
                        bass.AP(y_d, r_s2, [[S, 128], [128 * S, 8], [1, T]]),
                        fx_sb.ap())
                    loop_seal(LOOP_F, [(fO[0], fO[0].c)])
    return nc



def _host_inputs(L0_attn_x, L0_attn_qkv_w, L0_attn_out_w, L0_ffn_w1_w, L0_ffn_w1_s,
                 L0_ffn_w2_w, L0_ffn_w2_s, L0_ffn_h1_w, L0_ffn_h1_s, L0_ffn_h2_w,
                 L0_ffn_h2_s, L0_ffn_w3_w, L0_ffn_w3_s):
    x = np.ascontiguousarray(np.asarray(L0_attn_x, dtype=np.float32))
    cos128, sin128 = _rope_tables()
    shared = dict(
        qkvT=np.ascontiguousarray(np.asarray(L0_attn_qkv_w, np.float32).T),
        outT=np.ascontiguousarray(np.asarray(L0_attn_out_w, np.float32).T),
        tw1=_tape(np.asarray(L0_ffn_w1_w, np.float32)),
        tw2=_tape(np.asarray(L0_ffn_w2_w, np.float32)),
        th1=_tape(np.asarray(L0_ffn_h1_w, np.float32)),
        th2=_tape(np.asarray(L0_ffn_h2_w, np.float32)),
        tw3=_tape(np.asarray(L0_ffn_w3_w, np.float32)),
        sc1=_scale_cols(np.asarray(L0_ffn_w1_s, np.float32)),
        sc2=_scale_cols(np.asarray(L0_ffn_w2_s, np.float32)),
        sh1=_scale_cols(np.asarray(L0_ffn_h1_s, np.float32)),
        sh2=_scale_cols(np.asarray(L0_ffn_h2_s, np.float32)),
        sc3=_scale_cols(np.asarray(L0_ffn_w3_s, np.float32)),
        cosT=cos128, sinT=sin128, maskT=_mask_table(),
        ident=np.eye(128, dtype=np.float32),
        ones_c=np.ones((128, 1), dtype=np.float32),
        ones_r=np.ones((1, 128), dtype=np.float32),
        eps_c=np.full((1, 1), EPS, dtype=np.float32),
        zeros=np.zeros((128, 5504), dtype=np.float32),
    )
    in_maps = []
    for c in range(8):
        m = dict(shared)
        m["x"] = np.ascontiguousarray(x[c].T)
        in_maps.append(m)
    return in_maps


def kernel(**inputs):
    in_maps = _host_inputs(**inputs)
    nc = build_nc()
    res = run_bass_kernel_spmd(nc, in_maps, core_ids=list(range(8)))
    out = np.stack([np.ascontiguousarray(np.asarray(r["y"]).T) for r in res.results])
    return out.astype(np.float32)


# version 02
# speedup vs baseline: 1.2056x; 1.2056x over previous
import contextlib
import os
import numpy as np
import ml_dtypes
import concourse.bass as bass
import concourse.mybir as mybir
from concourse.bass_utils import run_bass_kernel_spmd
from concourse.alu_op_type import AluOpType as OP

F32 = mybir.dt.float32
ENG4 = None
BF16 = mybir.dt.bfloat16
AF = mybir.ActivationFunctionType
AX = mybir.AxisListType

D = 1024
S = 2048
NHEAD = 16
HD = 64
T = 512
NT = S // T
NSUB = T // 128
NEG = -1.0e38
EPS = 1.0e-5

BUILD_ATTN = True
BUILD_FFN = True

SWAP_MASK = []
for _i in range(16):
    SWAP_MASK += [2 * _i + 1, 2 * _i]


def _rope_tables():
    inv_freq = 1.0 / (10000.0 ** (np.arange(0, HD, 2, dtype=np.float64) / HD))
    pos = np.arange(S, dtype=np.float64)
    freqs = np.outer(pos, inv_freq)
    emb = np.concatenate([freqs, freqs], axis=-1)
    cos = np.cos(emb).astype(np.float32)
    sin = np.sin(emb).astype(np.float32)
    cosT = np.ascontiguousarray(cos.T)
    sinT = np.ascontiguousarray(sin.T)
    sinPM = sinT.copy()
    sinPM[0::2, :] *= -1.0
    cos128 = np.ascontiguousarray(np.concatenate([cosT, cosT], axis=0))
    sin128 = np.ascontiguousarray(np.concatenate([sinPM, sinPM], axis=0))
    return cos128, sin128


def _mask_table():
    m = np.zeros((16, 128, 260), dtype=np.float32)
    r = np.arange(128)[:, None]
    c = np.arange(128)[None, :]
    lower = (c <= r)
    upper = (c >= r)
    for t in range(16):
        m[t, :, 128:256] = np.where(lower, 0.0, NEG)
        if t == 0:
            m[t, :, 0:128] = NEG
        elif t % 2 == 1:
            m[t, :, 0:128] = 0.0
        else:
            m[t, :, 0:128] = np.where(upper, 0.0, NEG)
        m[t, :, 256:260] = 0.0 if t >= 2 else NEG
    return m


def _tape(w):
    MC = w.shape[0] // 128
    KC = w.shape[1] // 128
    t = w.reshape(MC, 128, KC, 128).transpose(0, 3, 2, 1)
    return np.ascontiguousarray(t.reshape(MC, 128, KC * 128))


def _scale_cols(s):
    MC = s.shape[0] // 128
    return np.ascontiguousarray(s.reshape(MC, 128).T)


class Sem:
    def __init__(self, nc, es, name):
        self.h = es.enter_context(nc.semaphore(name))
        self.c = 0

    def inc(self, n=1):
        self.c += n
        return self.c

    def reset(self):
        self.c = 0


def _hilo(vec, out_hi, out_lo, src, scale=None):
    vec.drain()
    if scale is None:
        vec.tensor_copy(out=out_hi, in_=src)
        vec.drain()
        return vec.scalar_tensor_tensor(out=out_lo, in0=src, scalar=1.0, in1=out_hi,
                                        op0=OP.mult, op1=OP.subtract)
    vec.tensor_scalar(out=out_hi, in0=src, scalar1=scale, scalar2=None, op0=OP.mult)
    vec.drain()
    return vec.scalar_tensor_tensor(out=out_lo, in0=src, scalar=scale, in1=out_hi,
                                    op0=OP.mult, op1=OP.subtract)



def _emit_ln(nc, x_sb, dst_hi, dst_lo, scr, consts, banks, sT, sV, sS, x_sem, x_cnt,
             w8=None):
    if w8 is None:
        def w8(eng, sem, cnt):
            eng.wait_ge(sem.h, cnt)
    m_sb, sq_sb, var_sb, sd_sb, a_sb, b_sb, tmpA, tmpB = scr
    ones_c, ones_r, eps_ap = consts
    bS1, bS2, bA, bB = banks
    base_T = sT.c
    base_S = sS.c
    for c in range(8):
        if c == 0:
            w8(nc.scalar, x_sem, x_cnt)
        else:
            w8(nc.scalar, sT, base_T + c)
        nc.scalar.activation(out=tmpA, in_=x_sb.ap()[:, c, :],
                             func=AF.Square).then_inc(sS.h, 1)
        sS.inc()
        w8(nc.tensor, sS, base_S + c + 1)
        nc.tensor.matmul(out=bS1.ap()[:1, :], lhsT=ones_c, rhs=x_sb.ap()[:, c, :],
                         start=(c == 0), stop=(c == 7))
        nc.tensor.matmul(out=bS2.ap()[:1, :], lhsT=ones_c, rhs=tmpA,
                         start=(c == 0), stop=(c == 7)).then_inc(sT.h, 1)
        sT.inc()
    w8(nc.vector, sT, sT.c)
    nc.vector.tensor_scalar(out=m_sb, in0=bS1.ap()[:1, :], scalar1=1.0 / D,
                            scalar2=None, op0=OP.mult)
    nc.vector.tensor_scalar(out=sq_sb, in0=bS2.ap()[:1, :], scalar1=1.0 / D,
                            scalar2=None, op0=OP.mult).then_inc(sV.h, 1)
    sV.inc()
    w8(nc.scalar, sV, sV.c)
    nc.scalar.activation(out=var_sb, in_=m_sb, func=AF.Square).then_inc(sS.h, 1)
    sS.inc()
    w8(nc.vector, sS, sS.c)
    nc.vector.tensor_sub(out=var_sb, in0=sq_sb, in1=var_sb).then_inc(sV.h, 1)
    sV.inc()
    w8(nc.scalar, sV, sV.c)
    nc.scalar.activation(out=sd_sb, in_=var_sb, func=AF.Sqrt, bias=eps_ap
                         ).then_inc(sS.h, 1)
    sS.inc()
    w8(nc.vector, sS, sS.c)
    nc.vector.reciprocal(out=a_sb, in_=sd_sb)
    nc.vector.drain()
    nc.vector.scalar_tensor_tensor(out=b_sb, in0=m_sb, scalar=-1.0, in1=a_sb,
                                   op0=OP.mult, op1=OP.mult).then_inc(sV.h, 1)
    sV.inc()
    w8(nc.tensor, sV, sV.c)
    nc.tensor.matmul(out=bA.ap()[:], lhsT=ones_r, rhs=a_sb, start=True, stop=True)
    nc.tensor.matmul(out=bB.ap()[:], lhsT=ones_r, rhs=b_sb, start=True,
                     stop=True).then_inc(sT.h, 1)
    sT.inc()
    w8(nc.vector, sT, sT.c)
    for c in range(8):
        nc.vector.tensor_mul(out=tmpB, in0=x_sb.ap()[:, c, :], in1=bA.ap()[:])
        nc.vector.drain()
        nc.vector.tensor_add(out=tmpA, in0=tmpB, in1=bB.ap()[:])
        _hilo(nc.vector, dst_hi.ap()[:, c, :], dst_lo.ap()[:, c, :], tmpA
              ).then_inc(sV.h, 1)
        sV.inc()



def _emit_attention_tile(nc, it, dma, loop_seal, loop_sems, sm, dr, b, pb, pt6, pt7,
                         regs):
    aX, aM, aQW, aOW = sm["aX"], sm["aM"], sm["aQW"], sm["aOW"]
    aT, aV, aSc, aSP = sm["aT"], sm["aV"], sm["aSc"], sm["aSP"]
    r_s1, r_s2, r_v1 = regs
    kbh = b["kb_hi"].ap()
    kbl = b["kb_lo"].ap()
    vbh = b["vb_hi"].ap()
    vbl = b["vb_lo"].ap()

    nc.sync.reg_mov(r_s1, it)
    nc.sync.reg_mul(r_s1, r_s1, T)
    dma(nc.sync, aX, b["x_sb"].ap(),
        bass.AP(dr["x_d"], r_s1, [[S, 128], [128 * S, 8], [1, T]]))
    dma(nc.sync, aM, b["ropeC"].ap(), bass.AP(dr["cos_d"], r_s1, [[S, 128], [1, T]]))
    dma(nc.sync, aM, b["ropeS"].ap(), bass.AP(dr["sin_d"], r_s1, [[S, 128], [1, T]]))
    nc.sync.reg_mov(r_s2, it)
    nc.sync.reg_mul(r_s2, r_s2, 4 * 128 * 260)
    dma(nc.sync, aM, b["mask_sb"].ap(),
        bass.AP(dr["mask_d"], r_s2, [[260, 128], [128 * 260, 4], [1, 260]]))

    ln_scr = (b["ln_m"].ap()[:], b["ln_sq"].ap()[:], b["ln_var"].ap()[:],
              b["ln_sd"].ap()[:], b["ln_a"].ap()[:], b["ln_b"].ap()[:],
              b["sc_u"].ap()[:], b["sc_qs"].ap()[:])
    _emit_ln(nc, b["x_sb"], b["xn_hi"], b["xn_lo"], ln_scr,
             (b["ones_c"].ap()[:], b["ones_r"].ap()[:], b["eps_sb"].ap()[:]),
             (pb[2], pb[3], pb[4], pb[5]), aT, aV, aSc, aX, 16)
    mark_ln = aV.c
    aT_after_ln = aT.c

    for g in range(8):
        if g >= 2:
            nc.sync.wait_ge(aT.h, aT_after_ln + 2 * (g - 1))
        dma(nc.sync, aQW[g % 2], b["qkw_hi"].ap()[:, g % 2],
            dr["qh_r"][:, :, 256 * g:256 * (g + 1)])
        dma(nc.sync, aQW[g % 2], b["qkw_lo"].ap()[:, g % 2],
            dr["ql_r"][:, :, 256 * g:256 * (g + 1)])

    mark_qk_evac = []
    for m in range(16):
        bank = pb[m % 2]
        grp = m // 2
        nc.tensor.wait_ge(aQW[grp % 2].h, 32 * (grp // 2 + 1))
        if m == 0:
            nc.tensor.wait_ge(aV.h, mark_ln)
        if m >= 2:
            nc.tensor.wait_ge(aV.h, mark_qk_evac[m - 2])
        for k in range(8):
            lh = b["qkw_hi"].ap()[:, grp % 2, k, (m % 2) * 128:(m % 2) * 128 + 128]
            ll = b["qkw_lo"].ap()[:, grp % 2, k, (m % 2) * 128:(m % 2) * 128 + 128]
            xh = b["xn_hi"].ap()[:, k, :]
            xl = b["xn_lo"].ap()[:, k, :]
            nc.tensor.matmul(out=bank.ap()[:], lhsT=lh, rhs=xh, start=(k == 0),
                             stop=False)
            nc.tensor.matmul(out=bank.ap()[:], lhsT=lh, rhs=xl, start=False, stop=False)
            mm = nc.tensor.matmul(out=bank.ap()[:], lhsT=ll, rhs=xh, start=False,
                                  stop=(k == 7))
        mm.then_inc(aT.h, 1)
        aT.inc()
        nc.vector.wait_ge(aT.h, aT.c)
        if m == 0:
            nc.vector.wait_ge(aM.h, 48)
        q32, qs, u = b["sc_q32"].ap()[:], b["sc_qs"].ap()[:], b["sc_u"].ap()[:]
        v2 = b["sc_v"].ap()[:]
        nc.vector.drain()
        nc.vector.tensor_copy(out=q32, in_=bank.ap()[:])
        nc.vector.drain()
        nc.vector.stream_shuffle(out=qs, in_=q32, mask=SWAP_MASK)
        nc.vector.tensor_mul(out=u, in0=q32, in1=b["ropeC"].ap()[:])
        nc.vector.drain()
        nc.vector.tensor_mul(out=v2, in0=qs, in1=b["ropeS"].ap()[:])
        nc.vector.drain()
        nc.vector.tensor_add(out=q32, in0=u, in1=v2)
        if m < 8:
            last = _hilo(nc.vector, b["qp_hi"].ap()[:, m, :], b["qp_lo"].ap()[:, m, :],
                         q32)
        else:
            c = m - 8
            for s in range(NSUB):
                slot = 128 * (s + 1)
                last = _hilo(nc.vector, kbh[:, c, slot:slot + 128],
                             kbl[:, c, slot:slot + 128],
                             b["sc_q32"].ap()[:, 128 * s:128 * (s + 1)])
        last.then_inc(aV.h, 1)
        aV.inc()
        mark_qk_evac.append(aV.c)
    nc.vector.drain()
    nc.vector.reg_mov(r_v1, it)
    with nc.vector.If_eq(r_v1, 0):
        for c in range(8):
            nc.vector.tensor_copy(out=kbh[:, c, 640:644], in_=kbh[:, c, 128:132])
            nc.vector.tensor_copy(out=kbl[:, c, 640:644], in_=kbl[:, c, 128:132])
    nc.vector.drain()
    nc.vector.sem_inc(aV.h, 1)
    aV.inc()
    mark_k_ready = aV.c

    mark_v_evac = []
    for s in range(NSUB):
        bank0, bank1 = (pb[2], pb[3]) if s % 2 == 0 else (pb[0], pb[1])
        if s == 1:
            nc.tensor.wait_ge(aV.h, mark_k_ready)
        if s >= 2:
            nc.tensor.wait_ge(aV.h, mark_v_evac[s - 2])
        for k in range(8):
            lhsTh = b["xn_hi"].ap()[:, k, 128 * s:128 * (s + 1)]
            lhsTl = b["xn_lo"].ap()[:, k, 128 * s:128 * (s + 1)]
            for half, bank in ((0, bank0), (1, bank1)):
                rh = b["vw_hi"].ap()[:, k, 512 * half:512 * (half + 1)]
                rl = b["vw_lo"].ap()[:, k, 512 * half:512 * (half + 1)]
                nc.tensor.matmul(out=bank.ap()[:], lhsT=lhsTh, rhs=rh,
                                 start=(k == 0), stop=False)
                nc.tensor.matmul(out=bank.ap()[:], lhsT=lhsTl, rhs=rh,
                                 start=False, stop=False)
                mm = nc.tensor.matmul(out=bank.ap()[:], lhsT=lhsTh, rhs=rl,
                                      start=False, stop=(k == 7))
        mm.then_inc(aT.h, 1)
        aT.inc()
        nc.vector.wait_ge(aT.h, aT.c)
        slot = s + 1
        _hilo(nc.vector, vbh[:, slot, 0:512], vbl[:, slot, 0:512], bank0.ap()[:])
        _hilo(nc.vector, vbh[:, slot, 512:1024], vbl[:, slot, 512:1024],
              bank1.ap()[:]).then_inc(aV.h, 1)
        aV.inc()
        mark_v_evac.append(aV.c)
    nc.vector.drain()
    nc.vector.reg_mov(r_v1, it)
    with nc.vector.If_eq(r_v1, 0):
        nc.vector.tensor_copy(out=b["vg_hi"].ap()[:, :], in_=vbh[0:4, 1, :])
        nc.vector.tensor_copy(out=b["vg_lo"].ap()[:, :], in_=vbl[0:4, 1, :])
    nc.vector.drain()
    nc.vector.sem_inc(aV.h, 1)
    aV.inc()
    mark_v_ready = aV.c

    heads = [(s, h) for s in range(NSUB) for h in range(NHEAD)]
    n_heads = len(heads)
    mark_sc = {}
    mark_tr = {}
    mark_pv = {}
    mark_mask = {}
    mark_p = {}
    mark_exp = {}
    mark_pte = {}
    mark_aoev = {}

    def emit_scores(n):
        s, h = heads[n]
        bank = pb[4 + (n % 2)]
        c = h // 2
        rb_ = 64 * (h % 2)
        if n == 0:
            nc.tensor.wait_ge(aV.h, mark_v_ready)
        if n >= 2:
            nc.tensor.wait_ge(aV.h, mark_mask[n - 2])
        qh = b["qp_hi"].ap()[rb_:rb_ + 64, c, 128 * s:128 * (s + 1)]
        ql = b["qp_lo"].ap()[rb_:rb_ + 64, c, 128 * s:128 * (s + 1)]
        kh_m = kbh[rb_:rb_ + 64, c, 128 * s:128 * s + 256]
        kl_m = kbl[rb_:rb_ + 64, c, 128 * s:128 * s + 256]
        kh_g = kbh[rb_:rb_ + 64, c, 640:644]
        kl_g = kbl[rb_:rb_ + 64, c, 640:644]
        o_m = bank.ap()[:, 0:256]
        o_g = bank.ap()[:, 256:260]
        nc.tensor.matmul(out=o_m, lhsT=qh, rhs=kh_m, start=True, stop=False)
        nc.tensor.matmul(out=o_g, lhsT=qh, rhs=kh_g, start=False, stop=False)
        nc.tensor.matmul(out=o_m, lhsT=qh, rhs=kl_m, start=False, stop=False)
        nc.tensor.matmul(out=o_g, lhsT=qh, rhs=kl_g, start=False, stop=False)
        nc.tensor.matmul(out=o_m, lhsT=ql, rhs=kh_m, start=False, stop=False)
        nc.tensor.matmul(out=o_g, lhsT=ql, rhs=kh_g, start=False,
                         stop=True).then_inc(aT.h, 1)
        aT.inc()
        mark_sc[n] = aT.c

    def emit_softmax_v(n):
        s, h = heads[n]
        par = n % 2
        bank = pb[4 + par]
        nc.vector.wait_ge(aT.h, mark_sc[n])
        nc.vector.tensor_add(out=b["s32"].ap()[:, par, :], in0=bank.ap()[:, 0:260],
                             in1=b["mask_sb"].ap()[:, s, :])
        nc.vector.drain()
        nc.vector.reduce_max(out=b["nmax"].ap()[:, par:par + 1],
                             in_=b["s32"].ap()[:, par, :], axis=AX.X,
                             negate=True).then_inc(aV.h, 1)
        aV.inc()
        mark_mask[n] = aV.c

    def emit_exp(n):
        par = n % 2
        nc.scalar.wait_ge(aV.h, mark_mask[n])
        nc.scalar.activation(out=b["p32"].ap()[:, par, :], in_=b["s32"].ap()[:, par, :],
                             func=AF.Exp, bias=b["nmax"].ap()[:, par:par + 1],
                             accum_out=b["sums"].ap()[:, par:par + 1]).then_inc(aSc.h, 1)
        aSc.inc()
        mark_exp[n] = aSc.c

    def emit_p(n):
        par = n % 2
        nc.vector.wait_ge(aSc.h, mark_exp[n])
        if n >= 2:
            nc.vector.wait_ge(aT.h, mark_tr[n - 2])
        nc.vector.reciprocal(out=b["rs"].ap()[:, par:par + 1],
                             in_=b["sums"].ap()[:, par:par + 1])
        nc.vector.drain()
        nc.vector.tensor_scalar(out=b["p_hi"].ap()[:, par, :],
                                in0=b["p32"].ap()[:, par, :],
                                scalar1=b["rs"].ap()[:, par:par + 1], scalar2=None,
                                op0=OP.mult)
        nc.vector.drain()
        nc.vector.scalar_tensor_tensor(out=b["p_lo"].ap()[:, par, :],
                                       in0=b["p32"].ap()[:, par, :],
                                       scalar=b["rs"].ap()[:, par:par + 1],
                                       in1=b["p_hi"].ap()[:, par, :],
                                       op0=OP.mult, op1=OP.subtract).then_inc(aV.h, 1)
        aV.inc()
        mark_p[n] = aV.c

    def emit_transposes(n):
        par = n % 2
        tp = pt6 if par == 0 else pt7
        nc.tensor.wait_ge(aV.h, mark_p[n])
        if n >= 2:
            nc.tensor.wait_ge(aSc.h, mark_pte[n - 2])
        idn = b["ident"].ap()[:]
        ph = b["p_hi"].ap()[:, par, :]
        pl = b["p_lo"].ap()[:, par, :]
        nc.tensor.transpose(out=tp.ap()[:, 0:128], in_=ph[:, 0:128], identity=idn)
        nc.tensor.transpose(out=tp.ap()[:, 128:256], in_=ph[:, 128:256], identity=idn)
        nc.tensor.transpose(out=tp.ap()[0:4, 256:384], in_=ph[:, 256:260], identity=idn)
        nc.tensor.transpose(out=tp.ap()[:, 384:512], in_=pl[:, 0:128], identity=idn)
        nc.tensor.transpose(out=tp.ap()[:, 512:640], in_=pl[:, 128:256], identity=idn)
        nc.tensor.transpose(out=tp.ap()[0:4, 640:768], in_=pl[:, 256:260],
                            identity=idn).then_inc(aT.h, 1)
        aT.inc()
        mark_tr[n] = aT.c

    def emit_pt_evac(n):
        par = n % 2
        tp = pt6 if par == 0 else pt7
        nc.scalar.wait_ge(aT.h, mark_tr[n])
        if n >= 2:
            nc.scalar.wait_ge(aT.h, mark_pv[n - 2])
        nc.scalar.copy(out=b["pt_hi"].ap()[:, par, 0:256], in_=tp.ap()[:, 0:256])
        nc.scalar.copy(out=b["pt_hi"].ap()[0:4, par, 256:384], in_=tp.ap()[0:4, 256:384])
        nc.scalar.copy(out=b["pt_lo"].ap()[:, par, 0:256], in_=tp.ap()[:, 384:640])
        nc.scalar.copy(out=b["pt_lo"].ap()[0:4, par, 256:384],
                       in_=tp.ap()[0:4, 640:768]).then_inc(aSc.h, 1)
        aSc.inc()
        mark_pte[n] = aSc.c

    def emit_pv(n):
        s, h = heads[n]
        par = n % 2
        pair = 8 * s + h // 2
        bank = pb[pair % 2]
        colbase = ((pair // 2) % 2) * 128
        rb_ = 64 * (h % 2)
        first = (h % 2 == 0)
        last = (h % 2 == 1)
        nc.tensor.wait_ge(aSc.h, mark_pte[n])
        if first and pair >= 2:
            nc.tensor.wait_ge(aV.h, mark_aoev[pair - 2])
        ph = b["pt_hi"].ap()[:, par, :]
        pl = b["pt_lo"].ap()[:, par, :]
        out_ap = bank.ap()[rb_:rb_ + 64, colbase:colbase + 128]
        for blk in range(2):
            vslot = s + blk
            vh = vbh[:, vslot, 64 * h:64 * h + 64]
            vl = vbl[:, vslot, 64 * h:64 * h + 64]
            rh = ph[:, 128 * blk:128 * (blk + 1)]
            rl = pl[:, 128 * blk:128 * (blk + 1)]
            nc.tensor.matmul(out=out_ap, lhsT=vh, rhs=rh, start=(blk == 0), stop=False)
            nc.tensor.matmul(out=out_ap, lhsT=vh, rhs=rl, start=False, stop=False)
            nc.tensor.matmul(out=out_ap, lhsT=vl, rhs=rh, start=False, stop=False)
        gh = b["vg_hi"].ap()[:, 64 * h:64 * h + 64]
        gl = b["vg_lo"].ap()[:, 64 * h:64 * h + 64]
        grh = ph[0:4, 256:384]
        grl = pl[0:4, 256:384]
        nc.tensor.matmul(out=out_ap, lhsT=gh, rhs=grh, start=False, stop=False)
        nc.tensor.matmul(out=out_ap, lhsT=gh, rhs=grl, start=False, stop=False)
        nc.tensor.matmul(out=out_ap, lhsT=gl, rhs=grh, start=False,
                         stop=True).then_inc(aT.h, 1)
        aT.inc()
        mark_pv[n] = aT.c
        if last:
            a_in_s = pair % 8
            nc.vector.wait_ge(aT.h, mark_pv[n])
            _hilo(nc.vector, b["ao_hi"].ap()[:, a_in_s, 128 * s:128 * (s + 1)],
                  b["ao_lo"].ap()[:, a_in_s, 128 * s:128 * (s + 1)],
                  bank.ap()[:, colbase:colbase + 128]).then_inc(aV.h, 1)
            aV.inc()
            mark_aoev[pair] = aV.c

    emit_scores(0)
    emit_softmax_v(0)
    emit_exp(0)
    emit_p(0)
    emit_scores(1)
    for n in range(n_heads):
        if n + 1 < n_heads:
            emit_softmax_v(n + 1)
            emit_exp(n + 1)
            emit_p(n + 1)
        if n + 2 < n_heads:
            emit_scores(n + 2)
        emit_transposes(n)
        emit_pt_evac(n)
        if n >= 1:
            emit_pv(n - 1)
    emit_pv(n_heads - 1)
    mark_all_ao = aV.c

    oh_r = dr["outh_d"].ap().rearrange("k p j -> p k j")
    ol_r = dr["outl_d"].ap().rearrange("k p j -> p k j")
    mark_op = []
    mark_res = []
    for m in range(8):
        if m >= 2:
            nc.sync.wait_ge(aT.h, mark_op[m - 2])
        dma(nc.sync, aOW[m % 2], b["ow_hi"].ap()[:, m % 2],
            oh_r[:, :, 128 * m:128 * (m + 1)])
        dma(nc.sync, aOW[m % 2], b["ow_lo"].ap()[:, m % 2],
            ol_r[:, :, 128 * m:128 * (m + 1)])
        bank = pb[2 + m % 2]
        nc.tensor.wait_ge(aOW[m % 2].h, 32 * (m // 2 + 1))
        if m == 0:
            nc.tensor.wait_ge(aV.h, mark_all_ao)
        if m >= 2:
            nc.tensor.wait_ge(aV.h, mark_res[m - 2])
        for k in range(8):
            lh = b["ow_hi"].ap()[:, m % 2, k, :]
            ll = b["ow_lo"].ap()[:, m % 2, k, :]
            rh = b["ao_hi"].ap()[:, k, :]
            rl = b["ao_lo"].ap()[:, k, :]
            nc.tensor.matmul(out=bank.ap()[:], lhsT=lh, rhs=rh, start=(k == 0),
                             stop=False)
            nc.tensor.matmul(out=bank.ap()[:], lhsT=lh, rhs=rl, start=False, stop=False)
            mm = nc.tensor.matmul(out=bank.ap()[:], lhsT=ll, rhs=rh, start=False,
                                  stop=(k == 7))
        mm.then_inc(aT.h, 1)
        aT.inc()
        mark_op.append(aT.c)
        nc.vector.wait_ge(aT.h, aT.c)
        nc.vector.tensor_add(out=b["x_sb"].ap()[:, m, :], in0=b["x_sb"].ap()[:, m, :],
                             in1=bank.ap()[:]).then_inc(aV.h, 1)
        aV.inc()
        mark_res.append(aV.c)

    for c in range(8):
        nc.vector.tensor_copy(out=kbh[:, c, 0:128], in_=kbh[:, c, 512:640])
        nc.vector.tensor_copy(out=kbl[:, c, 0:128], in_=kbl[:, c, 512:640])
    nc.vector.tensor_copy(out=vbh[:, 0, :], in_=vbh[:, 4, :])
    nc.vector.tensor_copy(out=vbl[:, 0, :], in_=vbl[:, 4, :]).then_inc(aV.h, 1)
    aV.inc()
    nc.sync.wait_ge(aV.h, aV.c)
    nc.sync.reg_mov(r_s1, it)
    nc.sync.reg_mul(r_s1, r_s1, T)
    dma(nc.sync, aSP, bass.AP(dr["x1s_d"], r_s1, [[S, 128], [128 * S, 8], [1, T]]),
        b["x_sb"].ap())

    loop_seal(loop_sems,
              [(aX, aX.c), (aM, aM.c), (aQW[0], aQW[0].c), (aQW[1], aQW[1].c),
               (aOW[0], aOW[0].c), (aOW[1], aOW[1].c), (aSP, aSP.c)])



def _emit_ffn_tile(nc, it, dma, loop_seal, loop_sems, sm, dr, b, pb, regs,
                   mono=None, aDone=None, in_loop=True):
    fX, fW, fT, fV, fS, fO = sm["fX"], sm["fW"], sm["fT"], sm["fV"], sm["fS"], sm["fO"]
    r_s1, r_s2, r_v1 = regs
    for s in (fX, fT, fV, fS, fO[0], fO[1], fW[0], fW[1]):
        s.reset()
    nc.gpsimd.wait_ge(aDone.h, 1)

    bases = mono["bases"]
    tmps = mono["tmps"]
    NAME = {id(fX): "fX", id(fT): "fT", id(fV): "fV", id(fS): "fS",
            id(fO[0]): "fO0", id(fO[1]): "fO1", id(fW[0]): "fW0",
            id(fW[1]): "fW1"}
    EKEY = {id(nc.sync): "sync", id(nc.tensor): "tensor", id(nc.vector): "vector",
            id(nc.scalar): "scalar", id(nc.gpsimd): "gpsimd"}

    def w8(eng, sem, cval):
        ek = EKEY[id(eng)]
        base = bases[(ek, NAME[id(sem)])]
        tmp = tmps[ek]
        eng.reg_mov(tmp, base)
        if cval:
            eng.reg_add(tmp, tmp, cval)
        eng.wait_ge(sem.h, tmp)

    nc.sync.reg_mov(r_s1, it)
    nc.sync.reg_mul(r_s1, r_s1, T)
    dma(nc.sync, fX, b["fx_sb"].ap(),
        bass.AP(dr["x1s_d"], r_s1, [[S, 128], [128 * S, 8], [1, T]]))

    ln_scr = (b["fln_m"].ap()[:], b["fln_sq"].ap()[:], b["fln_var"].ap()[:],
              b["fln_sd"].ap()[:], b["fln_a"].ap()[:], b["fln_b"].ap()[:],
              b["fact"].ap()[:, 0, :], b["fact"].ap()[:, 1, :])
    _emit_ln(nc, b["fx_sb"], b["fxn_hi"], b["fxn_lo"], ln_scr,
             (b["ones_c"].ap()[:], b["ones_r"].ap()[:], b["eps_sb"].ap()[:]),
             (pb[4], pb[5], pb[2], pb[3]), fT, fV, fS, fX, 16, w8=w8)
    mark_ln = fV.c

    wf = b["wf"].ap()
    state = {"in_v": mark_ln, "in_s": 0}

    def stage(name, nm, kc, src_hi, src_lo, tapes, emit_epilogue):
        stage_fT0 = fT.c
        mark_ep = []
        for m in range(nm):
            if m >= 2:
                w8(nc.gpsimd, fT, stage_fT0 + m - 1)
            else:
                w8(nc.gpsimd, fT, stage_fT0)
            for (ap_fn, off, cols) in tapes:
                dma(nc.gpsimd, fW[m % 2], wf[:, m % 2, off:off + cols], ap_fn(m))
            w8(nc.tensor, fW[m % 2], fW[m % 2].c)
            if m == 0:
                w8(nc.tensor, fV, state["in_v"])
                if state["in_s"]:
                    w8(nc.tensor, fS, state["in_s"])
            if m >= 2:
                w8(nc.tensor, fV, mark_ep[m - 2])
            if name == "s1":
                bankA, bankB = pb[2 * (m % 2)], pb[2 * (m % 2) + 1]
                for k in range(kc):
                    lw1 = wf[:, m % 2, 128 * k:128 * (k + 1)]
                    lw2 = wf[:, m % 2, 1024 + 128 * k:1024 + 128 * (k + 1)]
                    xh = src_hi[:, k, :]
                    xl = src_lo[:, k, :]
                    nc.tensor.matmul(out=bankA.ap()[:], lhsT=lw1, rhs=xh,
                                     start=(k == 0), stop=False)
                    nc.tensor.matmul(out=bankA.ap()[:], lhsT=lw1, rhs=xl,
                                     start=False, stop=(k == kc - 1))
                    nc.tensor.matmul(out=bankB.ap()[:], lhsT=lw2, rhs=xh,
                                     start=(k == 0), stop=False)
                    mm = nc.tensor.matmul(out=bankB.ap()[:], lhsT=lw2, rhs=xl,
                                          start=False, stop=(k == kc - 1))
                mm.then_inc(fT.h, 1)
                fT.inc()
                emit_epilogue(m, (bankA, bankB), mark_ep)
            else:
                bank = pb[m % 2]
                for k in range(kc):
                    lw = wf[:, m % 2, 128 * k:128 * (k + 1)]
                    xh = src_hi[:, k, :]
                    xl = src_lo[:, k, :]
                    nc.tensor.matmul(out=bank.ap()[:], lhsT=lw, rhs=xh,
                                     start=(k == 0), stop=False)
                    mm = nc.tensor.matmul(out=bank.ap()[:], lhsT=lw, rhs=xl,
                                          start=False, stop=(k == kc - 1))
                mm.then_inc(fT.h, 1)
                fT.inc()
                emit_epilogue(m, bank, mark_ep)
            mark_ep.append(fV.c)
        state["in_v"] = fV.c
        state["in_s"] = fS.c

    def epi1(m, banks, mark_ep):
        bankA, bankB = banks
        w8(nc.scalar, fT, fT.c)
        if m >= 2:
            w8(nc.scalar, fV, mark_ep[m - 2])
        nc.scalar.activation(out=b["fact"].ap()[:, m % 2, :], in_=bankA.ap()[:],
                             func=AF.Silu, scale=b["sc1"].ap()[:, m:m + 1]
                             ).then_inc(fS.h, 1)
        fS.inc()
        w8(nc.vector, fS, fS.c)
        nc.vector.scalar_tensor_tensor(out=b["fh32"].ap()[:, m % 2, :],
                                       in0=bankB.ap()[:],
                                       scalar=b["sc2"].ap()[:, m:m + 1],
                                       in1=b["fact"].ap()[:, m % 2, :],
                                       op0=OP.mult, op1=OP.mult)
        _hilo(nc.vector, b["h_hi"].ap()[:, m, :], b["h_lo"].ap()[:, m, :],
              b["fh32"].ap()[:, m % 2, :]).then_inc(fV.h, 1)
        fV.inc()

    def epi_gelu(dst_hi, dst_lo, ssc):
        def _e(m, bank, mark_ep):
            w8(nc.scalar, fT, fT.c)
            if m >= 2:
                w8(nc.scalar, fV, mark_ep[m - 2])
            nc.scalar.activation(out=b["fact"].ap()[:, m % 2, :], in_=bank.ap()[:],
                                 func=AF.Gelu, scale=ssc[:, m:m + 1]).then_inc(fS.h, 1)
            fS.inc()
            w8(nc.vector, fS, fS.c)
            _hilo(nc.vector, dst_hi[:, m, :], dst_lo[:, m, :],
                  b["fact"].ap()[:, m % 2, :]).then_inc(fV.h, 1)
            fV.inc()
        return _e

    def epi4(m, bank, mark_ep):
        w8(nc.vector, fT, fT.c)
        if m >= 2:
            w8(nc.vector, fO[m % 2], 16 * (m // 2))
        nc.vector.scalar_tensor_tensor(out=b["fo32"].ap()[:, m % 2, :], in0=bank.ap()[:],
                                       scalar=b["sc3"].ap()[:, m:m + 1],
                                       in1=b["fx_sb"].ap()[:, m, :],
                                       op0=OP.mult, op1=OP.add).then_inc(fV.h, 1)
        fV.inc()
        w8(nc.sync, fV, fV.c)
        nc.sync.reg_mov(r_s2, it)
        nc.sync.reg_mul(r_s2, r_s2, T)
        nc.sync.reg_add(r_s2, r_s2, m * 128 * S)
        dma(nc.sync, fO[m % 2], bass.AP(dr["y_d"], r_s2, [[S, 128], [1, T]]),
            b["fo32"].ap()[:, m % 2, :])

    n_stages = int(os.environ.get("MK_STAGES", "4"))
    if n_stages >= 1:
        stage("s1", 32, 8, b["fxn_hi"].ap(), b["fxn_lo"].ap(),
              [(lambda m: dr["tw1_d"].ap()[m], 0, 1024),
               (lambda m: dr["tw2_d"].ap()[m], 1024, 1024)], epi1)
    if n_stages >= 2:
        stage("s2", 32, 32, b["h_hi"].ap(), b["h_lo"].ap(),
              [(lambda m: dr["th1_d"].ap()[m], 0, 4096)],
              epi_gelu(b["g_hi"].ap(), b["g_lo"].ap(), b["sh1"].ap()))
    if n_stages >= 3:
        stage("s3", 32, 32, b["g_hi"].ap(), b["g_lo"].ap(),
              [(lambda m: dr["th2_d"].ap()[m], 0, 4096)],
              epi_gelu(b["h_hi"].ap(), b["h_lo"].ap(), b["sh2"].ap()))
    if n_stages >= 4:
        stage("s4", 8, 32, b["h_hi"].ap(), b["h_lo"].ap(),
              [(lambda m: dr["tw3_d"].ap()[m], 0, 4096)], epi4)
    else:
        nc.sync.wait_ge(fV.h, fV.c)
        nc.sync.reg_mov(r_s2, it)
        nc.sync.reg_mul(r_s2, r_s2, T)
        dma(nc.sync, fO[0], bass.AP(dr["y_d"], r_s2, [[S, 128], [128 * S, 8], [1, T]]),
            b["fx_sb"].ap())

    w8(nc.sync, fO[0], fO[0].c)
    w8(nc.sync, fO[1], fO[1].c)
    DELTA = {"fX": fX.c, "fT": fT.c, "fV": fV.c, "fS": fS.c, "fO0": fO[0].c,
             "fO1": fO[1].c, "fW0": fW[0].c, "fW1": fW[1].c}
    ENG = {"sync": nc.sync, "tensor": nc.tensor, "vector": nc.vector,
           "scalar": nc.scalar, "gpsimd": nc.gpsimd}
    for (ek, sname), reg in bases.items():
        ENG[ek].reg_add(reg, reg, DELTA[sname])
    mono["delta"] = DELTA
    nc.multi_engine_barrier([mybir.EngineType.SP, mybir.EngineType.PE,
                             mybir.EngineType.DVE, mybir.EngineType.Activation])



def build_nc():
    nc = bass.Bass("TRN2", target_bir_lowering=False)

    x_d = nc.dram_tensor("x", [D, S], F32, kind="ExternalInput")
    qkvT_d = nc.dram_tensor("qkvT", [D, 3 * D], F32, kind="ExternalInput")
    outT_d = nc.dram_tensor("outT", [D, D], F32, kind="ExternalInput")
    tw1_d = nc.dram_tensor("tw1", [32, 128, 1024], F32, kind="ExternalInput")
    tw2_d = nc.dram_tensor("tw2", [32, 128, 1024], F32, kind="ExternalInput")
    th1_d = nc.dram_tensor("th1", [32, 128, 4096], F32, kind="ExternalInput")
    th2_d = nc.dram_tensor("th2", [32, 128, 4096], F32, kind="ExternalInput")
    tw3_d = nc.dram_tensor("tw3", [8, 128, 4096], F32, kind="ExternalInput")
    sc1_d = nc.dram_tensor("sc1", [128, 32], F32, kind="ExternalInput")
    sc2_d = nc.dram_tensor("sc2", [128, 32], F32, kind="ExternalInput")
    sh1_d = nc.dram_tensor("sh1", [128, 32], F32, kind="ExternalInput")
    sh2_d = nc.dram_tensor("sh2", [128, 32], F32, kind="ExternalInput")
    sc3_d = nc.dram_tensor("sc3", [128, 8], F32, kind="ExternalInput")
    cos_d = nc.dram_tensor("cosT", [128, S], F32, kind="ExternalInput")
    sin_d = nc.dram_tensor("sinT", [128, S], F32, kind="ExternalInput")
    mask_d = nc.dram_tensor("maskT", [16, 128, 260], F32, kind="ExternalInput")
    ident_d = nc.dram_tensor("ident", [128, 128], F32, kind="ExternalInput")
    ones_c_d = nc.dram_tensor("ones_c", [128, 1], F32, kind="ExternalInput")
    ones_r_d = nc.dram_tensor("ones_r", [1, 128], F32, kind="ExternalInput")
    eps_d = nc.dram_tensor("eps_c", [1, 1], F32, kind="ExternalInput")
    zeros_d = nc.dram_tensor("zeros", [128, 5504], F32, kind="ExternalInput")
    y_d = nc.dram_tensor("y", [D, S], F32, kind="ExternalOutput")

    qkvh_d = nc.dram_tensor("qkv_hi", [8, 128, 3 * D], BF16, kind="Internal")
    qkvl_d = nc.dram_tensor("qkv_lo", [8, 128, 3 * D], BF16, kind="Internal")
    outh_d = nc.dram_tensor("out_hi", [8, 128, D], BF16, kind="Internal")
    outl_d = nc.dram_tensor("out_lo", [8, 128, D], BF16, kind="Internal")
    x1s_d = nc.dram_tensor("x1s", [D, S], F32, kind="Internal")

    es = contextlib.ExitStack()
    with es:
        pb = [es.enter_context(nc.psum_tensor(f"pb{j}", [128, 512], F32))
              for j in range(6)]
        pt6 = es.enter_context(nc.psum_tensor("pt6", [128, 1024], BF16))
        pt7 = es.enter_context(nc.psum_tensor("pt7", [128, 1024], BF16))

        pL = Sem(nc, es, "pL")
        pV = Sem(nc, es, "pV")
        pS = Sem(nc, es, "pS")
        pZ = Sem(nc, es, "pZ")
        aX = Sem(nc, es, "aX")
        aM = Sem(nc, es, "aM")
        aQW = (Sem(nc, es, "aQW0"), Sem(nc, es, "aQW1"))
        aOW = (Sem(nc, es, "aOW0"), Sem(nc, es, "aOW1"))
        aT = Sem(nc, es, "aT")
        aV = Sem(nc, es, "aV")
        aSc = Sem(nc, es, "aSc")
        aSP = Sem(nc, es, "aSP")
        aDone = Sem(nc, es, "aDone")
        fTm = Sem(nc, es, "fTm")
        LOOP_A = [aX, aM, aQW[0], aQW[1], aOW[0], aOW[1], aT, aV, aSc, aSP]
        fX = Sem(nc, es, "fX")
        fW = (Sem(nc, es, "fW0"), Sem(nc, es, "fW1"))
        fT = Sem(nc, es, "fT")
        fV = Sem(nc, es, "fV")
        fS = Sem(nc, es, "fS")
        fO = (Sem(nc, es, "fO0"), Sem(nc, es, "fO1"))
        LOOP_F = [fX, fT, fV, fS, fO[0], fO[1]]

        ident = es.enter_context(nc.sbuf_tensor("identC", [128, 128], BF16))
        ident32 = es.enter_context(nc.sbuf_tensor("identC32", [128, 128], F32))
        ones_c = es.enter_context(nc.sbuf_tensor("onesC", [128, 1], F32))
        ones_r = es.enter_context(nc.sbuf_tensor("onesR", [1, 128], F32))
        eps_sb = es.enter_context(nc.sbuf_tensor("epsC", [1, 1], F32))

        r_s1 = nc.sync.alloc_register("r_s1")
        r_s2 = nc.sync.alloc_register("r_s2")
        r_v1 = nc.vector.alloc_register("r_v1")
        regs = (r_s1, r_s2, r_v1)

        def dma(eng, sem, out, in_):
            eng.dma_start(out=out, in_=in_).then_inc(sem.h, 16)
            sem.inc(16)

        ENG4 = [mybir.EngineType.SP, mybir.EngineType.PE, mybir.EngineType.DVE,
                mybir.EngineType.Activation]

        def barrier4():
            nc.multi_engine_barrier(ENG4)

        def loop_seal(sems, dma_waits):
            for sem, cnt in dma_waits:
                nc.sync.wait_ge(sem.h, cnt)
            nc.all_engine_barrier()
            for sem in sems:
                nc.sync.sem_clear(sem.h)
                sem.reset()
            nc.all_engine_barrier()

        dma(nc.sync, pL, ident32[:], ident_d.ap())
        dma(nc.sync, pL, ones_c[:], ones_c_d.ap())
        dma(nc.sync, pL, ones_r[:], ones_r_d.ap())
        dma(nc.sync, pL, eps_sb[:], eps_d.ap())

        at_es = contextlib.ExitStack()
        x_sb = at_es.enter_context(nc.sbuf_tensor("x_sb", [128, 8, T], F32))
        xn_hi = at_es.enter_context(nc.sbuf_tensor("xn_hi", [128, 8, T], BF16))
        xn_lo = at_es.enter_context(nc.sbuf_tensor("xn_lo", [128, 8, T], BF16))
        qp_hi = at_es.enter_context(nc.sbuf_tensor("qp_hi", [128, 8, T], BF16))
        qp_lo = at_es.enter_context(nc.sbuf_tensor("qp_lo", [128, 8, T], BF16))
        kb_hi = at_es.enter_context(nc.sbuf_tensor("kb_hi", [128, 8, 644], BF16))
        kb_lo = at_es.enter_context(nc.sbuf_tensor("kb_lo", [128, 8, 644], BF16))
        vb_hi = at_es.enter_context(nc.sbuf_tensor("vb_hi", [128, 5, 1024], BF16))
        vb_lo = at_es.enter_context(nc.sbuf_tensor("vb_lo", [128, 5, 1024], BF16))
        vg_hi = at_es.enter_context(nc.sbuf_tensor("vg_hi", [4, 1024], BF16))
        vg_lo = at_es.enter_context(nc.sbuf_tensor("vg_lo", [4, 1024], BF16))
        ao_hi = at_es.enter_context(nc.sbuf_tensor("ao_hi", [128, 8, T], BF16))
        ao_lo = at_es.enter_context(nc.sbuf_tensor("ao_lo", [128, 8, T], BF16))
        vw_hi = at_es.enter_context(nc.sbuf_tensor("vw_hi", [128, 8, 1024], BF16))
        vw_lo = at_es.enter_context(nc.sbuf_tensor("vw_lo", [128, 8, 1024], BF16))
        qkw_hi = at_es.enter_context(nc.sbuf_tensor("qkw_hi", [128, 2, 8, 256], BF16))
        qkw_lo = at_es.enter_context(nc.sbuf_tensor("qkw_lo", [128, 2, 8, 256], BF16))
        ow_hi = at_es.enter_context(nc.sbuf_tensor("ow_hi", [128, 2, 8, 128], BF16))
        ow_lo = at_es.enter_context(nc.sbuf_tensor("ow_lo", [128, 2, 8, 128], BF16))
        ropeC = at_es.enter_context(nc.sbuf_tensor("ropeC", [128, T], F32))
        ropeS = at_es.enter_context(nc.sbuf_tensor("ropeS", [128, T], F32))
        mask_sb = at_es.enter_context(nc.sbuf_tensor("mask_sb", [128, 4, 260], F32))
        sc_q32 = at_es.enter_context(nc.sbuf_tensor("sc_q32", [128, T], F32))
        sc_qs = at_es.enter_context(nc.sbuf_tensor("sc_qs", [128, T], F32))
        sc_u = at_es.enter_context(nc.sbuf_tensor("sc_u", [128, T], F32))
        sc_v = at_es.enter_context(nc.sbuf_tensor("sc_v", [128, T], F32))
        s32 = at_es.enter_context(nc.sbuf_tensor("s32", [128, 2, 260], F32))
        p32 = at_es.enter_context(nc.sbuf_tensor("p32", [128, 2, 260], F32))
        p_hi = at_es.enter_context(nc.sbuf_tensor("p_hi", [128, 2, 260], BF16))
        p_lo = at_es.enter_context(nc.sbuf_tensor("p_lo", [128, 2, 260], BF16))
        pt_hi = at_es.enter_context(nc.sbuf_tensor("pt_hi", [128, 2, 384], BF16))
        pt_lo = at_es.enter_context(nc.sbuf_tensor("pt_lo", [128, 2, 384], BF16))
        nmax = at_es.enter_context(nc.sbuf_tensor("nmax", [128, 2], F32))
        sums = at_es.enter_context(nc.sbuf_tensor("sums", [128, 2], F32))
        rs = at_es.enter_context(nc.sbuf_tensor("rs", [128, 2], F32))
        ln_m = at_es.enter_context(nc.sbuf_tensor("ln_m", [1, T], F32))
        ln_sq = at_es.enter_context(nc.sbuf_tensor("ln_sq", [1, T], F32))
        ln_var = at_es.enter_context(nc.sbuf_tensor("ln_var", [1, T], F32))
        ln_sd = at_es.enter_context(nc.sbuf_tensor("ln_sd", [1, T], F32))
        ln_a = at_es.enter_context(nc.sbuf_tensor("ln_a", [1, T], F32))
        ln_b = at_es.enter_context(nc.sbuf_tensor("ln_b", [1, T], F32))

        with at_es:
            xflat = x_sb.ap().rearrange("p a b -> p (a b)")
            aoh_f = ao_hi.ap().rearrange("p a b -> p (a b)")
            aol_f = ao_lo.ap().rearrange("p a b -> p (a b)")
            qkvT = qkvT_d.ap()
            outT = outT_d.ap()

            nc.vector.wait_ge(pL.h, pL.c)
            nc.vector.tensor_copy(out=ident[:], in_=ident32[:]).then_inc(pV.h, 1)
            pV.inc()
            pV_base = pV.c
            pS_base = pS.c
            for k in range(8):
                nc.sync.wait_ge(pV.h, pV_base + k)
                dma(nc.sync, pL, xflat[:, 0:3 * D], qkvT[k * 128:(k + 1) * 128, :])
                nc.vector.wait_ge(pL.h, pL.c)
                if k > 0:
                    nc.vector.wait_ge(pS.h, pS_base + 32 * k)
                _hilo(nc.vector, aoh_f[:, 0:D], aol_f[:, 0:D], xflat[:, 0:D],
                      scale=0.125)
                _hilo(nc.vector, aoh_f[:, D:3 * D], aol_f[:, D:3 * D],
                      xflat[:, D:3 * D]).then_inc(pV.h, 1)
                pV.inc()
                nc.sync.wait_ge(pV.h, pV.c)
                dma(nc.sync, pS, qkvh_d.ap()[k], aoh_f[:, 0:3 * D])
                dma(nc.sync, pS, qkvl_d.ap()[k], aol_f[:, 0:3 * D])
            for k in range(8):
                nc.sync.wait_ge(pV.h, pV_base + 8 + k)
                dma(nc.sync, pL, xflat[:, 0:D], outT[k * 128:(k + 1) * 128, :])
                nc.vector.wait_ge(pL.h, pL.c)
                nc.vector.wait_ge(pS.h, pS_base + 256 + 32 * k)
                _hilo(nc.vector, aoh_f[:, 0:D], aol_f[:, 0:D],
                      xflat[:, 0:D]).then_inc(pV.h, 1)
                pV.inc()
                nc.sync.wait_ge(pV.h, pV.c)
                dma(nc.sync, pS, outh_d.ap()[k], aoh_f[:, 0:D])
                dma(nc.sync, pS, outl_d.ap()[k], aol_f[:, 0:D])

            nc.sync.wait_ge(pS.h, pS.c)
            qh_r = qkvh_d.ap().rearrange("k p j -> p k j")
            ql_r = qkvl_d.ap().rearrange("k p j -> p k j")
            dma(nc.sync, pL, vw_hi[:], qh_r[:, :, 2 * D:3 * D])
            dma(nc.sync, pL, vw_lo[:], ql_r[:, :, 2 * D:3 * D])
            zr = zeros_d.ap()
            kbh_f = kb_hi.ap().rearrange("p a b -> p (a b)")
            kbl_f = kb_lo.ap().rearrange("p a b -> p (a b)")
            vbh_f = vb_hi.ap().rearrange("p a b -> p (a b)")
            vbl_f = vb_lo.ap().rearrange("p a b -> p (a b)")
            dma(nc.gpsimd, pZ, kbh_f[:, :], zr[:, 0:8 * 644])
            dma(nc.gpsimd, pZ, kbl_f[:, :], zr[:, 0:8 * 644])
            dma(nc.gpsimd, pZ, vbh_f[:, :], zr[:, 0:5 * 1024])
            dma(nc.gpsimd, pZ, vbl_f[:, :], zr[:, 0:5 * 1024])
            dma(nc.gpsimd, pZ, vg_hi[:], zr[0:4, 0:1024])
            dma(nc.gpsimd, pZ, vg_lo[:], zr[0:4, 0:1024])
            nc.sync.wait_ge(pL.h, pL.c)
            nc.sync.wait_ge(pZ.h, pZ.c)
            nc.gpsimd.wait_ge(pZ.h, pZ.c)
            nc.all_engine_barrier()

            bufs_a = dict(x_sb=x_sb, xn_hi=xn_hi, xn_lo=xn_lo, qp_hi=qp_hi,
                          qp_lo=qp_lo, kb_hi=kb_hi, kb_lo=kb_lo, vb_hi=vb_hi,
                          vb_lo=vb_lo, vg_hi=vg_hi, vg_lo=vg_lo, ao_hi=ao_hi,
                          ao_lo=ao_lo, vw_hi=vw_hi, vw_lo=vw_lo, qkw_hi=qkw_hi,
                          qkw_lo=qkw_lo, ow_hi=ow_hi, ow_lo=ow_lo, ropeC=ropeC,
                          ropeS=ropeS, mask_sb=mask_sb, sc_q32=sc_q32, sc_qs=sc_qs,
                          sc_u=sc_u, sc_v=sc_v, s32=s32, p32=p32, p_hi=p_hi, p_lo=p_lo,
                          pt_hi=pt_hi, pt_lo=pt_lo, nmax=nmax, sums=sums, rs=rs,
                          ln_m=ln_m, ln_sq=ln_sq, ln_var=ln_var, ln_sd=ln_sd,
                          ln_a=ln_a, ln_b=ln_b, ident=ident, ones_c=ones_c,
                          ones_r=ones_r, eps_sb=eps_sb)
            dr_a = dict(x_d=x_d, cos_d=cos_d, sin_d=sin_d, mask_d=mask_d,
                        x1s_d=x1s_d, qh_r=qh_r, ql_r=ql_r, outh_d=outh_d,
                        outl_d=outl_d)
            sm_a = dict(aX=aX, aM=aM, aQW=aQW, aOW=aOW, aT=aT, aV=aV, aSc=aSc,
                        aSP=aSP)
            if BUILD_ATTN:
                with nc.Fori(0, NT) as it:
                    _emit_attention_tile(nc, it, dma, loop_seal, LOOP_A, sm_a, dr_a,
                                         bufs_a, pb, pt6, pt7, regs)
            else:
                with nc.Fori(0, NT) as it:
                    nc.sync.reg_mov(r_s1, it)
                    nc.sync.reg_mul(r_s1, r_s1, T)
                    dma(nc.sync, aX, x_sb.ap(),
                        bass.AP(x_d, r_s1, [[S, 128], [128 * S, 8], [1, T]]))
                    nc.sync.wait_ge(aX.h, aX.c)
                    nc.sync.reg_mov(r_s2, it)
                    nc.sync.reg_mul(r_s2, r_s2, T)
                    dma(nc.sync, aSP,
                        bass.AP(x1s_d, r_s2, [[S, 128], [128 * S, 8], [1, T]]),
                        x_sb.ap())
                    loop_seal(LOOP_A, [(aSP, aSP.c)])

        nc.sync.sem_inc(aDone.h, 1)

        ffn_es = contextlib.ExitStack()
        fx_sb = ffn_es.enter_context(nc.sbuf_tensor("fx_sb", [128, 8, T], F32))
        fxn_hi = ffn_es.enter_context(nc.sbuf_tensor("fxn_hi", [128, 8, T], BF16))
        fxn_lo = ffn_es.enter_context(nc.sbuf_tensor("fxn_lo", [128, 8, T], BF16))
        h_hi = ffn_es.enter_context(nc.sbuf_tensor("h_hi", [128, 32, T], BF16))
        h_lo = ffn_es.enter_context(nc.sbuf_tensor("h_lo", [128, 32, T], BF16))
        g_hi = ffn_es.enter_context(nc.sbuf_tensor("g_hi", [128, 32, T], BF16))
        g_lo = ffn_es.enter_context(nc.sbuf_tensor("g_lo", [128, 32, T], BF16))
        wf = ffn_es.enter_context(nc.sbuf_tensor("wf", [128, 2, 4096], BF16))
        sc1 = ffn_es.enter_context(nc.sbuf_tensor("sc1s", [128, 32], F32))
        sc2 = ffn_es.enter_context(nc.sbuf_tensor("sc2s", [128, 32], F32))
        sh1 = ffn_es.enter_context(nc.sbuf_tensor("sh1s", [128, 32], F32))
        sh2 = ffn_es.enter_context(nc.sbuf_tensor("sh2s", [128, 32], F32))
        sc3 = ffn_es.enter_context(nc.sbuf_tensor("sc3s", [128, 8], F32))
        fact = ffn_es.enter_context(nc.sbuf_tensor("fact", [128, 2, T], F32))
        fh32 = ffn_es.enter_context(nc.sbuf_tensor("fh32", [128, 2, T], F32))
        fo32 = ffn_es.enter_context(nc.sbuf_tensor("fo32", [128, 2, T], F32))
        fln_m = ffn_es.enter_context(nc.sbuf_tensor("fln_m", [1, T], F32))
        fln_sq = ffn_es.enter_context(nc.sbuf_tensor("fln_sq", [1, T], F32))
        fln_var = ffn_es.enter_context(nc.sbuf_tensor("fln_var", [1, T], F32))
        fln_sd = ffn_es.enter_context(nc.sbuf_tensor("fln_sd", [1, T], F32))
        fln_a = ffn_es.enter_context(nc.sbuf_tensor("fln_a", [1, T], F32))
        fln_b = ffn_es.enter_context(nc.sbuf_tensor("fln_b", [1, T], F32))

        with ffn_es:
            bufs_f = dict(fx_sb=fx_sb, fxn_hi=fxn_hi, fxn_lo=fxn_lo, h_hi=h_hi,
                          h_lo=h_lo, g_hi=g_hi, g_lo=g_lo, wf=wf, sc1=sc1, sc2=sc2,
                          sh1=sh1, sh2=sh2, sc3=sc3, fact=fact, fh32=fh32,
                          fo32=fo32, fln_m=fln_m, fln_sq=fln_sq, fln_var=fln_var,
                          fln_sd=fln_sd, fln_a=fln_a, fln_b=fln_b, ones_c=ones_c,
                          ones_r=ones_r, eps_sb=eps_sb)
            dr_f = dict(x1s_d=x1s_d, y_d=y_d, tw1_d=tw1_d, tw2_d=tw2_d, th1_d=th1_d,
                        th2_d=th2_d, tw3_d=tw3_d)
            sm_f = dict(fX=fX, fW=fW, fT=fT, fV=fV, fS=fS, fO=fO)
            NOLOOP = os.environ.get("MK_NOLOOP", "0") == "1"
            if BUILD_FFN:
                dma(nc.sync, pL, sc1[:], sc1_d.ap())
                dma(nc.sync, pL, sc2[:], sc2_d.ap())
                dma(nc.sync, pL, sh1[:], sh1_d.ap())
                dma(nc.sync, pL, sh2[:], sh2_d.ap())
                dma(nc.sync, pL, sc3[:], sc3_d.ap())
                nc.sync.wait_ge(pL.h, pL.c)
                barrier4()
                PAIRS = [("scalar", "fX"), ("scalar", "fT"), ("scalar", "fV"),
                         ("tensor", "fS"), ("tensor", "fV"), ("tensor", "fW0"),
                         ("tensor", "fW1"), ("vector", "fT"), ("vector", "fS"),
                         ("vector", "fO0"), ("vector", "fO1"), ("sync", "fV"),
                         ("sync", "fO0"), ("sync", "fO1"), ("gpsimd", "fT")]
                ENG = {"sync": nc.sync, "tensor": nc.tensor, "vector": nc.vector,
                       "scalar": nc.scalar, "gpsimd": nc.gpsimd}
                bases = {}
                for ek, sname in PAIRS:
                    r = ENG[ek].alloc_register(f"mb_{ek}_{sname}")
                    ENG[ek].reg_mov(r, 0)
                    bases[(ek, sname)] = r
                tmps = {}
                for ek in ENG:
                    tmps[ek] = ENG[ek].alloc_register(f"mt_{ek}")
                mono = dict(bases=bases, tmps=tmps)
                if NOLOOP:
                    for it in range(int(os.environ.get("MK_NTILES", "1"))):
                        _emit_ffn_tile(nc, it, dma, loop_seal, LOOP_F, sm_f, dr_f,
                                       bufs_f, pb, regs, mono=mono,
                                       aDone=aDone, in_loop=False)
                else:
                    with nc.Fori(0, NT) as it:
                        _emit_ffn_tile(nc, it, dma, loop_seal, LOOP_F, sm_f, dr_f,
                                       bufs_f, pb, regs, mono=mono,
                                       aDone=aDone, in_loop=True)
                d = mono["delta"]
                niters = int(os.environ.get("MK_NTILES", "1")) if NOLOOP else NT
                nc.sync.wait_ge(fO[0].h, niters * d["fO0"])
                nc.sync.wait_ge(fO[1].h, niters * d["fO1"])
            else:
                with nc.Fori(0, NT) as it:
                    nc.sync.reg_mov(r_s1, it)
                    nc.sync.reg_mul(r_s1, r_s1, T)
                    dma(nc.sync, fX, fx_sb.ap(),
                        bass.AP(x1s_d, r_s1, [[S, 128], [128 * S, 8], [1, T]]))
                    nc.sync.wait_ge(fX.h, fX.c)
                    nc.sync.reg_mov(r_s2, it)
                    nc.sync.reg_mul(r_s2, r_s2, T)
                    dma(nc.sync, fO[0],
                        bass.AP(y_d, r_s2, [[S, 128], [128 * S, 8], [1, T]]),
                        fx_sb.ap())
                    loop_seal(LOOP_F, [(fO[0], fO[0].c)])
    return nc



def _host_inputs(L0_attn_x, L0_attn_qkv_w, L0_attn_out_w, L0_ffn_w1_w, L0_ffn_w1_s,
                 L0_ffn_w2_w, L0_ffn_w2_s, L0_ffn_h1_w, L0_ffn_h1_s, L0_ffn_h2_w,
                 L0_ffn_h2_s, L0_ffn_w3_w, L0_ffn_w3_s):
    x = np.ascontiguousarray(np.asarray(L0_attn_x, dtype=np.float32))
    cos128, sin128 = _rope_tables()
    shared = dict(
        qkvT=np.ascontiguousarray(np.asarray(L0_attn_qkv_w, np.float32).T),
        outT=np.ascontiguousarray(np.asarray(L0_attn_out_w, np.float32).T),
        tw1=_tape(np.asarray(L0_ffn_w1_w, np.float32)),
        tw2=_tape(np.asarray(L0_ffn_w2_w, np.float32)),
        th1=_tape(np.asarray(L0_ffn_h1_w, np.float32)),
        th2=_tape(np.asarray(L0_ffn_h2_w, np.float32)),
        tw3=_tape(np.asarray(L0_ffn_w3_w, np.float32)),
        sc1=_scale_cols(np.asarray(L0_ffn_w1_s, np.float32)),
        sc2=_scale_cols(np.asarray(L0_ffn_w2_s, np.float32)),
        sh1=_scale_cols(np.asarray(L0_ffn_h1_s, np.float32)),
        sh2=_scale_cols(np.asarray(L0_ffn_h2_s, np.float32)),
        sc3=_scale_cols(np.asarray(L0_ffn_w3_s, np.float32)),
        cosT=cos128, sinT=sin128, maskT=_mask_table(),
        ident=np.eye(128, dtype=np.float32),
        ones_c=np.ones((128, 1), dtype=np.float32),
        ones_r=np.ones((1, 128), dtype=np.float32),
        eps_c=np.full((1, 1), EPS, dtype=np.float32),
        zeros=np.zeros((128, 5504), dtype=np.float32),
    )
    in_maps = []
    for c in range(8):
        m = dict(shared)
        m["x"] = np.ascontiguousarray(x[c].T)
        in_maps.append(m)
    return in_maps


def kernel(**inputs):
    in_maps = _host_inputs(**inputs)
    nc = build_nc()
    res = run_bass_kernel_spmd(nc, in_maps, core_ids=list(range(8)))
    out = np.stack([np.ascontiguousarray(np.asarray(r["y"]).T) for r in res.results])
    return out.astype(np.float32)
